# version 01
# baseline (speedup 1.0000x reference)
import numpy as np
import ml_dtypes
from contextlib import ExitStack

import concourse.bass as bass
import concourse.mybir as mybir
from concourse.bass_utils import run_bass_kernel_spmd

F32 = mybir.dt.float32
BF16 = mybir.dt.bfloat16
AF = mybir.ActivationFunctionType
ALU = mybir.AluOpType
AX = mybir.AxisListType

B, T, D, H, Dh = 16, 2048, 1024, 16, 64
NCORES = 8
SEQS = 2
NTOK = SEQS * T
NP = H // 2
KC = D // 128
SH_M = 22
NE = 16
EPS = 1e-5

_bf16 = ml_dtypes.bfloat16



def _perm_eo(block64):
    return np.concatenate([block64[0::2], block64[1::2]], axis=0)


def prep_weights(inp):
    w_qkv = np.asarray(inp["L0_attn_w_qkv"], dtype=np.float32)
    b_qkv = np.asarray(inp["L0_attn_b_qkv"], dtype=np.float32)
    w_out = np.asarray(inp["L0_attn_w_out"], dtype=np.float32)
    b_out = np.asarray(inp["L0_attn_b_out"], dtype=np.float32)
    gate_w = np.asarray(inp["L0_ffn_gate_w"], dtype=np.float32)
    w_gu = np.asarray(inp["L0_ffn_w_gate_up"], dtype=np.float32)
    w_dn = np.asarray(inp["L0_ffn_w_down"], dtype=np.float32)
    swg = np.asarray(inp["L0_ffn_shared_w_gate"], dtype=np.float32)
    swu = np.asarray(inp["L0_ffn_shared_w_up"], dtype=np.float32)
    swd = np.asarray(inp["L0_ffn_shared_w_down"], dtype=np.float32)
    sgw = np.asarray(inp["L0_ffn_shared_gate_w"], dtype=np.float32)

    wq, wk, wv = w_qkv[0:D], w_qkv[D:2 * D], w_qkv[2 * D:3 * D]
    bq, bk, bv = b_qkv[0:D], b_qkv[D:2 * D], b_qkv[2 * D:3 * D]

    aqk = np.empty((NP, 128, KC, 2, 128), dtype=np.float32)
    bqk = np.empty((128, 2 * NP), dtype=np.float32)
    for p in range(NP):
        a, b_ = 2 * p, 2 * p + 1
        for t_i, (w, bias) in enumerate(((wq, bq), (wk, bk))):
            ea = _perm_eo(w[64 * a:64 * a + 64])
            eb = _perm_eo(w[64 * b_:64 * b_ + 64])
            rows = np.concatenate([ea[0:32], eb[0:32], ea[32:64], eb[32:64]],
                                  axis=0)
            ba = _perm_eo(bias[64 * a:64 * a + 64])
            bb = _perm_eo(bias[64 * b_:64 * b_ + 64])
            bqk[:, 2 * p + t_i] = np.concatenate(
                [ba[0:32], bb[0:32], ba[32:64], bb[32:64]], axis=0)
            for kc in range(KC):
                aqk[p, :, kc, t_i, :] = rows[:, 128 * kc:128 * kc + 128].T

    av = np.empty((NP, 128, KC, 128), dtype=np.float32)
    for p in range(NP):
        cols = wv[128 * p:128 * p + 128]
        for kc in range(KC):
            av[p, :, kc, :] = cols[:, 128 * kc:128 * kc + 128].T
    bvh = np.empty((128, KC), dtype=np.float32)
    for kc in range(KC):
        bvh[:, kc] = bv[128 * kc:128 * kc + 128]

    awout = np.empty((128, KC, D), dtype=np.float32)
    for kc in range(KC):
        awout[:, kc, :] = w_out[:, 128 * kc:128 * kc + 128].T

    g17 = np.empty((128, KC, 17), dtype=np.float32)
    for kc in range(KC):
        g17[:, kc, 0:16] = gate_w[:, 128 * kc:128 * kc + 128].T
        g17[:, kc, 16] = sgw[0, 128 * kc:128 * kc + 128]

    agu = np.empty((NE, 4, 128, KC, 2, 128), dtype=np.float32)
    for e in range(NE):
        for mp in range(4):
            for kc in range(KC):
                agu[e, mp, :, kc, 0, :] = w_gu[e, 128 * kc:128 * kc + 128,
                                               128 * mp:128 * mp + 128]
                agu[e, mp, :, kc, 1, :] = w_gu[e, 128 * kc:128 * kc + 128,
                                               512 + 128 * mp:
                                               512 + 128 * mp + 128]

    ad = np.ascontiguousarray(np.transpose(
        w_dn.reshape(NE, 4, 128, D), (0, 2, 1, 3)))

    asg = np.empty((SH_M, 128, KC, 128), dtype=np.float32)
    asu = np.empty((SH_M, 128, KC, 128), dtype=np.float32)
    for m in range(SH_M):
        for kc in range(KC):
            asg[m, :, kc, :] = swg[128 * m:128 * m + 128,
                                   128 * kc:128 * kc + 128].T
            asu[m, :, kc, :] = swu[128 * m:128 * m + 128,
                                   128 * kc:128 * kc + 128].T
    asd = np.ascontiguousarray(np.transpose(swd.reshape(D, SH_M, 128),
                                            (2, 1, 0)))

    invf = 10000.0 ** (-(np.arange(0, Dh, 2, dtype=np.float64) / Dh))
    fr = invf[:, None] * np.arange(T, dtype=np.float64)[None, :]
    c32 = np.cos(fr).astype(np.float32)
    s32 = np.sin(fr).astype(np.float32)
    css = np.ascontiguousarray(np.concatenate([c32, c32, s32, s32], axis=0))
    ssc = np.ascontiguousarray(np.concatenate([s32, s32, c32, c32], axis=0))

    return {
        "aqk": np.ascontiguousarray(aqk), "bqk": bqk,
        "av": np.ascontiguousarray(av), "bvh": bvh,
        "awout": np.ascontiguousarray(awout),
        "bout": np.ascontiguousarray(b_out[None, :]),
        "g17": np.ascontiguousarray(g17),
        "agu": np.ascontiguousarray(agu), "ad": ad,
        "asg": np.ascontiguousarray(asg), "asu": np.ascontiguousarray(asu),
        "asd": asd, "css": css, "ssc": ssc,
    }



ENGINES = ("tensor", "vector", "scalar", "gpsimd", "sync")


class Sem:
    def __init__(self, nc, name):
        self.h = nc.alloc_semaphore(name=name)
        self.v = 0

    def inc(self, n=1):
        self.v += n
        return self.v


class MK:
    def __init__(self, nc):
        self.nc = nc
        self.ops = {e: [] for e in ENGINES}
        self.s_w = Sem(nc, "s_wdma")
        self.s_x = Sem(nc, "s_xdma")
        self.s_st = Sem(nc, "s_store")
        self.s_pe = Sem(nc, "s_pe")
        self.s_act = Sem(nc, "s_act")
        self.s_dve = Sem(nc, "s_dve")

    def op(self, eng, fn, sem=None, inc=1):
        if sem is None:
            self.ops[eng].append(fn)
            return None
        v = sem.inc(inc)
        self.ops[eng].append(lambda e, fn=fn, s=sem.h, i=inc:
                             fn(e).then_inc(s, i))
        return (sem, v)

    def wait(self, eng, g):
        if not g:
            return
        sem, v = g
        if v <= 0:
            return
        self.ops[eng].append(lambda e, s=sem.h, v=v: e.wait_ge(s, v))

    def flush(self):
        ops = self.ops
        self.ops = {e: [] for e in ENGINES}

        def runner(lst):
            def f(eng):
                for o in lst:
                    o(eng)
            return f
        with self.nc.Block() as blk:
            for eng in ENGINES:
                if ops[eng]:
                    getattr(blk, eng)(runner(ops[eng]))


def _layernorm_tile(mk, src, dst_bf16, stats, sqdump):
    va = mk.op("scalar", lambda e, s=src, st=stats, d=sqdump: e.activation(
        d[:], s[:], AF.Square, accum_out=st[:, 1:2]), mk.s_act)
    vb = mk.op("vector", lambda e, s=src, st=stats: e.reduce_sum(
        st[:, 0:1], s[:], AX.X), mk.s_dve)
    mk.wait("vector", va)
    mk.op("vector", lambda e, st=stats: e.tensor_scalar_mul(
        st[:, 2:3], st[:, 0:1], 1.0 / D), None)
    vc = mk.op("vector", lambda e, st=stats: e.tensor_scalar_mul(
        st[:, 3:4], st[:, 1:2], 1.0 / D), mk.s_dve)
    mk.wait("vector", vc)
    vd = mk.op("vector", lambda e, st=stats: e.tensor_mul(
        st[:, 4:5], st[:, 2:3], st[:, 2:3]), mk.s_dve)
    mk.wait("vector", vd)
    ve = mk.op("vector", lambda e, st=stats: e.tensor_scalar(
        st[:, 5:6], st[:, 3:4], st[:, 4:5], float(EPS), ALU.subtract,
        ALU.add), mk.s_dve)
    mk.wait("scalar", ve)
    vf = mk.op("scalar", lambda e, st=stats: e.activation(
        st[:, 5:6], st[:, 5:6], AF.Sqrt), mk.s_act)
    mk.wait("vector", vf)
    vg = mk.op("vector", lambda e, st=stats: e.reciprocal(
        st[:, 6:7], st[:, 5:6]), mk.s_dve)
    mk.wait("vector", vg)
    vh = mk.op("vector", lambda e, st=stats: e.tensor_scalar(
        st[:, 7:8], st[:, 2:3], st[:, 6:7], -1.0, ALU.mult, ALU.mult),
        mk.s_dve)
    mk.wait("scalar", vh)
    vi = mk.op("scalar", lambda e, s=src, d=dst_bf16, st=stats: e.activation(
        d[:], s[:], AF.Identity, bias=st[:, 7:8], scale=st[:, 6:7]), mk.s_act)
    return vi


def _transpose_tile(mk, ln_tm, fm_dst, i, ptA, ptB, ident, pe_guard):
    last = None
    for g, pt in ((0, ptA), (1, ptB)):
        mk.wait("tensor", pe_guard[g])
        for c4 in range(4):
            c = 4 * g + c4
            mk.op("tensor", lambda e, c=c, c4=c4, pt=pt, ln=ln_tm, idn=ident:
                  e.transpose(pt[:, 128 * c4:128 * c4 + 128],
                              ln[:, 128 * c:128 * c + 128], idn[:]),
                  mk.s_pe if c4 == 3 else None)
        vpe = (mk.s_pe, mk.s_pe.v)
        mk.wait("scalar", vpe)
        v = mk.op("scalar", lambda e, g=g, pt=pt, i=i, d=fm_dst: e.copy(
            d[:, 4 * g:4 * g + 4, 128 * i:128 * i + 128], pt[:]), mk.s_act)
        pe_guard[g] = v
        last = v
    return last


def build_nc(taps=(), stop_after=None):
    nc = bass.Bass("TRN2", target_bir_lowering=False)
    taps = set(taps)

    dt_in = [
        ("x", [NTOK, D], F32), ("aqk", [NP, 128, KC, 2, 128], F32),
        ("bqk", [128, 2 * NP], F32), ("av", [NP, 128, KC, 128], F32),
        ("bvh", [128, KC], F32), ("awout", [128, KC, D], F32),
        ("bout", [1, D], F32), ("g17", [128, KC, 17], F32),
        ("agu", [NE, 4, 128, KC, 2, 128], F32), ("ad", [NE, 128, 4, D], F32),
        ("asg", [SH_M, 128, KC, 128], F32), ("asu", [SH_M, 128, KC, 128], F32),
        ("asd", [128, SH_M, D], F32), ("css", [128, T], F32),
        ("ssc", [128, T], F32),
    ]
    dr = {n: nc.dram_tensor(n, s, d, kind="ExternalInput").ap()
          for n, s, d in dt_in}
    y = nc.dram_tensor("y", [NTOK, D], F32, kind="ExternalOutput").ap()
    x2hbm = nc.dram_tensor("x2hbm", [NTOK, D], F32, kind="Internal").ap()
    ln2hbm = nc.dram_tensor("ln2hbm", [128, KC, T], BF16, kind="Internal").ap()

    tap_t = {}
    tap_specs = {
        "ln1": ("t_ln1", [128, KC, T], BF16),
        "qrot": ("t_qrot", [128, T], BF16), "krot": ("t_krot", [128, T], BF16),
        "attn": ("t_attn", [128, KC, T], BF16),
        "ln2": ("t_ln2", [128, KC, T], BF16),
        "ctm": ("t_ctm", [128, 32, 16], F32), "gtm": ("t_gtm", [128, 32], F32),
        "outacc": ("t_outacc", [128, 16, D], F32),
        "vtap": ("t_vtap", [128, 16, 65], BF16),
        "x2": ("t_x2", [NTOK, D], F32),
        "shfm": ("t_shfm", [128, 8, T], BF16),
        "outacc1": ("t_outacc1", [128, 16, D], F32),
        "ln2ffn": ("t_ln2ffn", [128, KC, T], BF16),
        "gtm2": ("t_gtm2", [128, 32], F32),
        "dbg1": ("t_dbg1", [128, 1], F32),
        "dbg2": ("t_dbg2", [128, 512], BF16),
        "dbg3": ("t_dbg3", [1, 512], F32),
        "dbg4": ("t_dbg4", [32, 8], F32),
        "dbg5": ("t_dbg5", [2, T], F32),
    }
    for k in taps:
        name, shape, dt = tap_specs[k]
        tap_t[k] = nc.dram_tensor(name, shape, dt, kind="ExternalOutput").ap()

    ident_d = nc.inline_tensor(np.eye(128, dtype=_bf16), name="identc").ap()
    onesrow_d = nc.inline_tensor(np.ones((1, 128), dtype=_bf16),
                                 name="onesrowc").ap()
    o2 = np.zeros((128, 2), dtype=_bf16)
    o2[0:64, 0] = 1
    o2[64:128, 1] = 1
    ones2_d = nc.inline_tensor(o2, name="ones2c").ap()
    identf_d = nc.inline_tensor(np.eye(128, dtype=np.float32),
                                name="identfc").ap()
    _r = np.arange(128)
    m4q_d = nc.inline_tensor(
        np.where(_r[None, :] > _r[:, None], -1e9, 0.0).astype(np.float32),
        name="m4qc").ap()
    m4k_d = nc.inline_tensor(
        np.where(_r[:, None] > _r[None, :], -1e9, 0.0).astype(np.float32),
        name="m4kc").ap()

    mk = MK(nc)
    sb = nc.sbuf_tensor
    ps = nc.psum_tensor
    with ExitStack() as es:
        ident = es.enter_context(sb("ident", [128, 128], BF16))
        onesrow = es.enter_context(sb("onesrow", [1, 128], BF16))
        ones2 = es.enter_context(sb("ones2", [128, 2], BF16))
        bqk_sb = es.enter_context(sb("bqk_sb", [128, 2 * NP], F32))
        bfull = es.enter_context(sb("bfull", [128, D], F32))
        g17_sb = es.enter_context(sb("g17_sb", [128, KC, 17], BF16))
        c_tm = es.enter_context(sb("c_tm", [128, 32, 16], F32))
        g_tm = es.enter_context(sb("g_tm", [128, 32], F32))
        ln2buf = es.enter_context(sb("ln2buf", [128, KC, T], BF16))

        with ExitStack() as ss:
            awout_sb = ss.enter_context(sb("awout0", [128, KC, D], BF16))
            bvh_sb = ss.enter_context(sb("bvh_sb", [128, KC], F32))
            bvb = ss.enter_context(sb("bvb", [128, KC], BF16))
            bout_sb = ss.enter_context(sb("bout_sb", [1, D], F32))
            brow = ss.enter_context(sb("brow", [1, D], F32))
            browb = ss.enter_context(sb("browb", [1, D], BF16))
            pb = ss.enter_context(ps("pb", [128, 512], F32))
            pb2 = ss.enter_context(ps("pb2", [128, 512], F32))

            mk.op("sync", lambda e: e.dma_start(ident[:], ident_d), mk.s_x, 16)
            vor = mk.op("sync", lambda e: e.dma_start(onesrow[:], onesrow_d),
                        mk.s_x, 16)
            mk.op("sync", lambda e: e.dma_start(ones2[:], ones2_d),
                  mk.s_x, 16)
            mk.op("sync", lambda e: e.dma_start(bqk_sb[:], dr["bqk"]),
                  mk.s_x, 16)
            mk.op("gpsimd", lambda e: e.dma_start(g17_sb[:], dr["g17"]),
                  mk.s_w, 16)
            va = mk.op("gpsimd", lambda e: e.dma_start(awout_sb[:],
                                                       dr["awout"]),
                       mk.s_w, 16)
            vb = mk.op("sync", lambda e: e.dma_start(bvh_sb[:], dr["bvh"]),
                       mk.s_x, 16)
            vo = mk.op("sync", lambda e: e.dma_start(bout_sb[:], dr["bout"]),
                       mk.s_x, 16)
            mk.wait("scalar", vb)
            vc = mk.op("scalar", lambda e: e.copy(bvb[:], bvh_sb[:]),
                       mk.s_act)
            mk.wait("tensor", vc)
            mk.wait("tensor", va)
            for half in range(2):
                dst = pb if half == 0 else pb2
                for kc in range(KC):
                    mk.op("tensor", lambda e, kc=kc, h=half, d=dst: e.matmul(
                        d[0:1, :], bvb[:, kc:kc + 1],
                        awout_sb[:, kc, 512 * h:512 * h + 512],
                        start=(kc == 0), stop=(kc == KC - 1)),
                        mk.s_pe if kc == KC - 1 else None)
            vpe = (mk.s_pe, mk.s_pe.v)
            mk.wait("vector", vpe)
            mk.wait("vector", vo)
            mk.op("vector", lambda e: e.tensor_add(
                brow[0:1, 0:512], pb[0:1, :], bout_sb[0:1, 0:512]), None)
            vd = mk.op("vector", lambda e: e.tensor_add(
                brow[0:1, 512:1024], pb2[0:1, :], bout_sb[0:1, 512:1024]),
                mk.s_dve)
            mk.wait("scalar", vd)
            vc2 = mk.op("scalar", lambda e: e.copy(browb[:], brow[:]),
                        mk.s_act)
            mk.wait("tensor", vc2)
            mk.wait("tensor", vor)
            mk.wait("tensor", vd)
            for half in range(2):
                dst = pb if half == 0 else pb2
                mk.op("tensor", lambda e, h=half, d=dst: e.matmul(
                    d[:], onesrow[0:1, :], browb[0:1, 512 * h:512 * h + 512],
                    start=True, stop=True), mk.s_pe)
            vpe2 = (mk.s_pe, mk.s_pe.v)
            mk.wait("scalar", vpe2)
            mk.op("scalar", lambda e: e.copy(bfull[:, 0:512], pb[:]),
                  mk.s_act)
            mk.op("scalar", lambda e: e.copy(bfull[:, 512:1024], pb2[:]),
                  mk.s_act)
            mk.flush()

        with ExitStack() as att:
            css = att.enter_context(sb("css_sb", [128, T], F32))
            ssc = att.enter_context(sb("ssc_sb", [128, T], F32))
            identf = att.enter_context(sb("identf", [128, 128], F32))
            m4q_sb = att.enter_context(sb("m4q_sb", [128, 128], F32))
            m4k_sb = att.enter_context(sb("m4k_sb", [128, 128], F32))
            ln1_fm = att.enter_context(sb("ln1_fm", [128, KC, T], BF16))
            attn_fm = att.enter_context(sb("attn_fm", [128, KC, T], BF16))

            mk.op("sync", lambda e: e.dma_start(css[:], dr["css"]),
                  mk.s_x, 16)
            mk.op("sync", lambda e: e.dma_start(ssc[:], dr["ssc"]),
                  mk.s_x, 16)
            mk.op("sync", lambda e: e.dma_start(identf[:], identf_d),
                  mk.s_x, 16)
            mk.op("sync", lambda e: e.dma_start(m4q_sb[:], m4q_d),
                  mk.s_x, 16)
            mk.op("sync", lambda e: e.dma_start(m4k_sb[:], m4k_d),
                  mk.s_x, 16)
            mk.flush()

            for seq in range(SEQS):
                _attn_seq(mk, sb, ps, dr, seq, css, ssc, m4q_sb, m4k_sb,
                          identf, ln1_fm, attn_fm, ln2buf, bqk_sb, bfull,
                          ident, onesrow, x2hbm, ln2hbm, tap_t, taps,
                          stop_after)
                if stop_after in ("ln1", "attn") and seq == 0:
                    break

        if stop_after in ("ln1", "attn", "a3"):
            if "x2" in taps and stop_after == "a3":
                mk.op("sync", lambda e: e.dma_start(tap_t["x2"], x2hbm),
                      mk.s_st, 16)
            mk.wait("sync", (mk.s_st, mk.s_st.v))
            mk.flush()
            return nc

        with ExitStack() as fes:
            _ffn_phase(mk, fes, sb, ps, dr, ln2buf, ln2hbm, g17_sb, c_tm,
                       g_tm, x2hbm, y, tap_t, taps, stop_after)

        if "x2" in taps:
            mk.op("sync", lambda e: e.dma_start(tap_t["x2"], x2hbm),
                  mk.s_st, 16)
        mk.wait("sync", (mk.s_st, mk.s_st.v))
        mk.flush()
    return nc


def _attn_seq(mk, sb, ps, dr, seq, css, ssc, m4q_sb, m4k_sb, identf,
              ln1_fm, attn_fm, ln2buf, bqk_sb, bfull, ident, onesrow,
              x2hbm, ln2hbm, tap_t, taps, stop_after):
    tok0 = seq * T

    with ExitStack() as ss:
        xt = [ss.enter_context(sb(f"xt{seq}_{i}", [128, D], F32))
              for i in range(2)]
        lnt = [ss.enter_context(sb(f"lnt{seq}_{i}", [128, D], BF16))
               for i in range(2)]
        stats = [ss.enter_context(sb(f"lnstat{seq}_{i}", [128, 8], F32))
                 for i in range(2)]
        sqdump = ss.enter_context(sb(f"sqd{seq}", [128, D], BF16))
        ptA = ss.enter_context(ps(f"ptA{seq}", [128, 512], BF16))
        ptB = ss.enter_context(ps(f"ptB{seq}", [128, 512], BF16))
        pe_guard = {0: 0, 1: 0}
        buf_free = [0, 0]
        ln_free = [0, 0]
        for i in range(T // 128):
            par = i % 2
            mk.wait("sync", buf_free[par])
            vx = mk.op("sync", lambda e, i=i, par=par, xt=xt, t0=tok0:
                       e.dma_start(xt[par][:],
                                   dr["x"][t0 + 128 * i:t0 + 128 * i + 128,
                                           :]), mk.s_x, 16)
            mk.wait("scalar", vx)
            mk.wait("vector", vx)
            mk.wait("scalar", ln_free[par])
            vi = _layernorm_tile(mk, xt[par], lnt[par], stats[par], sqdump)
            buf_free[par] = vi
            mk.wait("tensor", vi)
            vtr = _transpose_tile(mk, lnt[par], ln1_fm, i, ptA, ptB, ident,
                                  pe_guard)
            ln_free[par] = vtr
        mk.flush()

    if "ln1" in taps and seq == 0:
        mk.op("sync", lambda e: e.dma_start(tap_t["ln1"], ln1_fm[:]),
              mk.s_st, 16)
        mk.wait("sync", (mk.s_st, mk.s_st.v))
        mk.flush()
    if stop_after == "ln1":
        return
    with ExitStack() as ss:
        wqk = [ss.enter_context(sb(f"wqk{seq}_{i}", [128, KC, 2, 128], BF16))
               for i in range(2)]
        avp = [ss.enter_context(sb(f"avp{seq}_{i}", [128, KC, 128], BF16))
               for i in range(2)]
        qst = ss.enter_context(sb(f"qst{seq}", [128, 512], F32))
        kst = ss.enter_context(sb(f"kst{seq}", [128, 512], F32))
        mt1 = ss.enter_context(sb(f"mt1{seq}", [128, 512], F32))
        mt2 = ss.enter_context(sb(f"mt2{seq}", [128, 512], F32))
        mo = ss.enter_context(sb(f"mo{seq}", [64, 512], F32))
        me = ss.enter_context(sb(f"me{seq}", [64, 512], F32))
        qrot = ss.enter_context(sb(f"qrot{seq}", [128, T], BF16))
        krot = ss.enter_context(sb(f"krot{seq}", [128, T], BF16))
        qag = [ss.enter_context(sb(f"qag{seq}_{h}", [65, T], BF16))
               for h in range(2)]
        kag = [ss.enter_context(sb(f"kag{seq}_{h}", [65, T], BF16))
               for h in range(2)]
        macc = ss.enter_context(sb(f"macc{seq}", [128, 4], F32))
        cmx = ss.enter_context(sb(f"cmx{seq}", [128, 1], F32))
        vh = [ss.enter_context(sb(f"vh{seq}_{h}", [128, 16, 65], BF16))
              for h in range(2)]
        prb = [ss.enter_context(sb(f"prb{seq}_{hj}", [128, 512], BF16))
               for hj in range(4)]
        invzr = ss.enter_context(sb(f"invzr{seq}", [1, 512], F32))
        invzb = ss.enter_context(sb(f"invzb{seq}", [1, 512], BF16))
        invzbc = ss.enter_context(sb(f"invzbc{seq}", [64, 512], F32))
        pq = ss.enter_context(ps(f"pq{seq}", [128, 512], F32))
        pk = ss.enter_context(ps(f"pk{seq}", [128, 512], F32))
        psA = [ss.enter_context(ps(f"psA{seq}_{j}", [128, 512], F32))
               for j in range(2)]
        psB = [ss.enter_context(ps(f"psB{seq}_{j}", [128, 512], F32))
               for j in range(2)]
        ppv = [ss.enter_context(ps(f"ppv{seq}_{h}", [65, 512], F32))
               for h in range(2)]

        mk.op("vector", lambda e, v=vh: e.memset(v[0][:, :, 64:65], 1.0),
              None)
        mk.op("vector", lambda e, v=vh: e.memset(v[1][:, :, 64:65], 1.0),
              None)
        mk.op("vector", lambda e, k=kag: e.memset(k[0][64:65, :], 1.0), None)
        mk.op("vector", lambda e, k=kag: e.memset(k[1][64:65, :], 1.0),
              mk.s_dve)

        g_wqk = [0, 0]
        g_avp = [0, 0]
        g_pq = 0
        g_pk = 0
        g_psA = [0, 0]
        g_psB = [0, 0]
        g_ppv = [0, 0]
        g_prb = [0, 0, 0, 0]
        g_rot_pe = 0
        g_aug_pe = 0
        g_vh = 0
        g_stage = 0
        g_invzr = 0
        g_invzbc = 0
        g_macc = 0

        for p in range(NP):
            par = p % 2
            mk.wait("gpsimd", g_wqk[par])
            vw = mk.op("gpsimd", lambda e, p=p, par=par, w=wqk: e.dma_start(
                w[par][:], dr["aqk"][p]), mk.s_w, 16)
            mk.wait("gpsimd", g_avp[par])
            vav = mk.op("gpsimd", lambda e, p=p, par=par, a=avp: e.dma_start(
                a[par][:], dr["av"][p]), mk.s_w, 16)

            mk.wait("tensor", vw)
            for n in range(4):
                for t_i in range(2):
                    dst = pq if t_i == 0 else pk
                    mk.wait("tensor", g_pq if t_i == 0 else g_pk)
                    for kc in range(KC):
                        mk.op("tensor", lambda e, n=n, kc=kc, t_i=t_i,
                              par=par, dst=dst, w=wqk, lf=ln1_fm: e.matmul(
                                  dst[:], w[par][:, kc, t_i, :],
                                  lf[:, kc, 512 * n:512 * n + 512],
                                  start=(kc == 0), stop=(kc == KC - 1)),
                              mk.s_pe if kc == KC - 1 else None)
                    vmm = (mk.s_pe, mk.s_pe.v)
                    mk.wait("scalar", vmm)
                    mk.wait("scalar", g_stage)
                    dstf = qst if t_i == 0 else kst
                    bia = 2 * p + t_i
                    vev = mk.op("scalar", lambda e, dstf=dstf, bia=bia,
                                dst=dst, bq=bqk_sb: e.activation(
                                    dstf[:], dst[:],
                                    AF.Identity, bias=bq[:, bia:bia + 1]),
                                mk.s_act)
                    if t_i == 0:
                        g_pq = vev
                    else:
                        g_pk = vev
                mk.wait("vector", g_pk)
                if n == 0:
                    mk.wait("vector", g_rot_pe)
                for srcf, dstrot in ((qst, qrot), (kst, krot)):
                    sl = slice(512 * n, 512 * n + 512)
                    mk.op("vector", lambda e, s=srcf, sl=sl, m=mt1, c=css:
                          e.tensor_mul(m[:], s[:], c[:, sl]), None)
                    v1 = mk.op("vector", lambda e, s=srcf, sl=sl, m=mt2,
                               c=ssc: e.tensor_mul(m[:], s[:], c[:, sl]),
                               mk.s_dve)
                    mk.wait("vector", v1)
                    mk.op("vector", lambda e, m=mt1, o=mo: e.tensor_copy(
                        o[:], m[64:128, :]), None)
                    v2 = mk.op("vector", lambda e, m=mt2, o=me: e.tensor_copy(
                        o[:], m[64:128, :]), mk.s_dve)
                    mk.wait("vector", v2)
                    mk.op("vector", lambda e, d=dstrot, sl=sl, m=mt1, o=mo:
                          e.tensor_sub(d[0:32, sl], m[0:32, :], o[0:32, :]),
                          None)
                    mk.op("vector", lambda e, d=dstrot, sl=sl, m=mt2, o=me:
                          e.tensor_add(d[32:64, sl], m[0:32, :], o[0:32, :]),
                          None)
                    mk.op("vector", lambda e, d=dstrot, sl=sl, m=mt1, o=mo:
                          e.tensor_sub(d[64:96, sl], m[32:64, :],
                                       o[32:64, :]), None)
                    v3 = mk.op("vector", lambda e, d=dstrot, sl=sl, m=mt2,
                               o=me: e.tensor_add(d[96:128, sl], m[32:64, :],
                                                  o[32:64, :]), mk.s_dve)
                    mk.wait("vector", v3)
                g_stage = (mk.s_dve, mk.s_dve.v)
            rope_done = (mk.s_dve, mk.s_dve.v)

            mk.wait("vector", g_aug_pe)
            mk.op("vector", lambda e, q=qag, r=qrot: e.tensor_copy(
                q[0][0:64, :], r[0:64, :]), None)
            mk.op("vector", lambda e, q=qag, r=qrot: e.tensor_copy(
                q[1][0:64, :], r[64:128, :]), None)
            mk.op("vector", lambda e, k=kag, r=krot: e.tensor_copy(
                k[0][0:64, :], r[0:64, :]), None)
            vcp = mk.op("vector", lambda e, k=kag, r=krot: e.tensor_copy(
                k[1][0:64, :], r[64:128, :]), mk.s_dve)
            aug_rows_done = vcp

            mk.wait("tensor", vav)
            mk.wait("vector", g_vh)
            for i in range(T // 128):
                dstp = psB[i % 2]
                mk.wait("tensor", g_psB[i % 2])
                for kc in range(KC):
                    mk.op("tensor", lambda e, i=i, kc=kc, par=par, dstp=dstp,
                          a=avp, lf=ln1_fm: e.matmul(
                              dstp[:, 0:128],
                              lf[:, kc, 128 * i:128 * i + 128],
                              a[par][:, kc, :],
                              start=(kc == 0), stop=(kc == KC - 1)),
                          mk.s_pe if kc == KC - 1 else None)
                vv = (mk.s_pe, mk.s_pe.v)
                mk.wait("vector", vv)
                mk.op("vector", lambda e, i=i, dstp=dstp, v=vh:
                      e.tensor_copy(v[0][:, i, 0:64], dstp[:, 0:64]), None)
                g_psB[i % 2] = mk.op(
                    "vector", lambda e, i=i, dstp=dstp, v=vh: e.tensor_copy(
                        v[1][:, i, 0:64], dstp[:, 64:128]), mk.s_dve)
            g_avp[par] = (mk.s_pe, mk.s_pe.v)
            v_done = (mk.s_dve, mk.s_dve.v)

            if seq == 0 and p == 0:
                if "vtap" in taps:
                    mk.wait("sync", v_done)
                    mk.op("sync", lambda e, v=vh: e.dma_start(tap_t["vtap"],
                                                              v[0][:]),
                          mk.s_st, 16)
                if "qrot" in taps:
                    mk.wait("sync", rope_done)
                    mk.op("sync", lambda e: e.dma_start(tap_t["qrot"],
                                                        qrot[:]), mk.s_st, 16)
                    mk.op("sync", lambda e: e.dma_start(tap_t["krot"],
                                                        krot[:]), mk.s_st, 16)

            if p == 0 and seq == 0:
                mk.wait("vector", (mk.s_x, mk.s_x.v))
            for i in range(T // 128):
                nk = i // 4 + 1
                u_d = i % 4
                for h in range(2):
                    mk.wait("vector", g_macc)
                    for n in range(nk):
                        jp = n % 2
                        pst = psA if h == 0 else psB
                        mk.wait("tensor", g_psA[jp] if h == 0 else
                                g_psB[jp])
                        if i == 0 and n == 0:
                            mk.wait("tensor", rope_done)
                        tp = (0, 0) if h == 0 else (64, 0)
                        r0 = 64 * h
                        mk.op("tensor", lambda e, i=i, n=n, r0=r0, tp=tp,
                              dstp=pst[jp], qr=qrot, kr=krot: e.matmul(
                                  dstp[:], qr[r0:r0 + 64,
                                              128 * i:128 * i + 128],
                                  kr[r0:r0 + 64, 512 * n:512 * n + 512],
                                  start=True, stop=True, tile_position=tp),
                              mk.s_pe)
                        vqk = (mk.s_pe, mk.s_pe.v)
                        mk.wait("vector", vqk)
                        if n == i // 4:
                            vmk = mk.op("vector", lambda e, u=u_d,
                                        dstp=pst[jp], m=m4q_sb:
                                        e.tensor_add(
                                            dstp[:, 128 * u:128 * u + 128],
                                            dstp[:, 128 * u:128 * u + 128],
                                            m[:]), mk.s_dve)
                            mk.wait("vector", vmk)
                            red_sl = slice(0, 128 * (u_d + 1))
                        else:
                            red_sl = slice(0, 512)
                        if n == 0:
                            vred = mk.op("vector", lambda e, h=h, sl=red_sl,
                                         dstp=pst[jp], mc=macc: e.reduce_max(
                                             mc[:, h:h + 1], dstp[:, sl],
                                             AX.X), mk.s_dve)
                        else:
                            vred = mk.op("vector", lambda e, sl=red_sl,
                                         dstp=pst[jp], c=cmx: e.reduce_max(
                                             c[:], dstp[:, sl], AX.X),
                                         mk.s_dve)
                            mk.wait("vector", vred)
                            vred = mk.op("vector", lambda e, h=h, mc=macc,
                                         c=cmx: e.tensor_max(
                                             mc[:, h:h + 1], mc[:, h:h + 1],
                                             c[:]), mk.s_dve)
                        if h == 0:
                            g_psA[jp] = vred
                        else:
                            g_psB[jp] = vred
                        mk.wait("vector", vred)
                vng = mk.op("vector", lambda e, mc=macc: e.tensor_scalar_mul(
                    mc[:, 2:4], mc[:, 0:2], -1.0), mk.s_dve)
                mk.wait("tensor", vng)
                mk.wait("tensor", g_pk)
                for h in range(2):
                    mk.op("tensor", lambda e, h=h, mc=macc, d=pk, idf=identf,
                          i=i: e.transpose(d[0:1, 128 * h:128 * h + 128],
                                           mc[:, 2 + h:3 + h], idf[:]),
                          mk.s_pe)
                vtr = (mk.s_pe, mk.s_pe.v)
                g_macc = vtr
                mk.wait("scalar", vtr)
                mk.op("scalar", lambda e, i=i, q=qag, d=pk: e.copy(
                    q[0][64:65, 128 * i:128 * i + 128], d[0:1, 0:128]), None)
                vev = mk.op("scalar", lambda e, i=i, q=qag, d=pk: e.copy(
                    q[1][64:65, 128 * i:128 * i + 128], d[0:1, 128:256]),
                    mk.s_act)
                g_pk = vev
            aug_done = (mk.s_act, mk.s_act.v)

            for s in range(4):
                nj = 4 * s + 4
                mk.wait("tensor", g_ppv[0])
                mk.wait("tensor", g_ppv[1])
                for j in range(nj):
                    jp = j % 2
                    diag_r = j - 4 * s
                    for h in range(2):
                        pst = psA if h == 0 else psB
                        mk.wait("tensor", g_psA[jp] if h == 0 else
                                g_psB[jp])
                        if j == 0 and h == 0 and s == 0:
                            mk.wait("tensor", aug_done)
                            mk.wait("tensor", aug_rows_done)
                            mk.wait("tensor", (mk.s_dve,
                                               mk.s_dve.v))
                        mk.op("tensor", lambda e, j=j, s=s, h=h,
                              dstp=pst[jp], ka=kag, qa=qag: e.matmul(
                                  dstp[:], ka[h][:, 128 * j:128 * j + 128],
                                  qa[h][:, 512 * s:512 * s + 512],
                                  start=True, stop=True), mk.s_pe)
                        vqk = (mk.s_pe, mk.s_pe.v)
                        pbuf = prb[2 * h + jp]
                        mk.wait("scalar", vqk)
                        mk.wait("scalar", g_prb[2 * h + jp])
                        if diag_r >= 0:
                            mk.wait("vector", vqk)
                            mk.wait("vector", g_prb[2 * h + jp])
                            vdg = mk.op("vector", lambda e, r=diag_r,
                                        dstp=pst[jp], m=m4k_sb:
                                        e.tensor_add(
                                            dstp[:, 128 * r:128 * r + 128],
                                            dstp[:, 128 * r:128 * r + 128],
                                            m[:]), mk.s_dve)
                            if diag_r > 0:
                                vz = mk.op("vector", lambda e, r=diag_r,
                                           pbuf=pbuf: e.memset(
                                               pbuf[:, 0:128 * r], 0.0),
                                           mk.s_dve)
                            mk.wait("scalar", vdg)
                            if diag_r > 0:
                                mk.wait("scalar", vz)
                            sl_e = slice(128 * diag_r, 512)
                        else:
                            sl_e = slice(0, 512)
                        vex = mk.op("scalar", lambda e, pbuf=pbuf, sl=sl_e,
                                    dstp=pst[jp]: e.activation(
                                        pbuf[:, sl], dstp[:, sl], AF.Exp,
                                        scale=0.125), mk.s_act)
                        if h == 0:
                            g_psA[jp] = vex
                        else:
                            g_psB[jp] = vex
                        prb_ready = vex
                        if (seq == 0 and p == 0 and s == 0 and j == 0
                                and h == 0 and "dbg2" in taps):
                            mk.wait("sync", prb_ready)
                            mk.op("sync", lambda e, pbuf=pbuf: e.dma_start(
                                tap_t["dbg2"], pbuf[:]), mk.s_st, 16)
                        mk.wait("tensor", prb_ready)
                        mk.op("tensor", lambda e, h=h, j=j, nj=nj, pbuf=pbuf,
                              v=vh, pv=ppv: e.matmul(
                                  pv[h][:], v[h][:, j, :], pbuf[:],
                                  start=(j == 0), stop=(j == nj - 1)),
                              mk.s_pe)
                        g_prb[2 * h + jp] = (mk.s_pe, mk.s_pe.v)
                        if j == nj - 1:
                            vpv_done = (mk.s_pe, mk.s_pe.v)
                            mk.wait("vector", vpv_done)
                            mk.wait("vector", g_invzr)
                            vrc = mk.op("vector", lambda e, h=h, pv=ppv,
                                        iz=invzr: e.reciprocal(
                                            iz[0:1, :], pv[h][64:65, :]),
                                        mk.s_dve)
                            mk.wait("scalar", vrc)
                            vcb = mk.op("scalar", lambda e, iz=invzr,
                                        ib=invzb: e.copy(ib[:], iz[:]),
                                        mk.s_act)
                            g_invzr = vcb
                            if (seq == 0 and p == 0 and s == 0 and h == 0
                                    and "dbg3" in taps):
                                mk.wait("sync", vcb)
                                mk.op("sync", lambda e: e.dma_start(
                                    tap_t["dbg3"], invzr[:]), mk.s_st, 16)
                            mk.wait("tensor", vcb)
                            mk.wait("tensor", g_pq)
                            mk.op("tensor", lambda e, o=onesrow, ib=invzb,
                                  d=pq: e.matmul(d[0:64, :], o[0:1, 0:64],
                                                 ib[0:1, :], start=True,
                                                 stop=True), mk.s_pe)
                            vbc = (mk.s_pe, mk.s_pe.v)
                            mk.wait("scalar", vbc)
                            mk.wait("scalar", g_invzbc)
                            vzc = mk.op("scalar", lambda e, d=pq, z=invzbc:
                                        e.copy(z[:], d[0:64, :]), mk.s_act)
                            g_pq = vzc
                            mk.wait("vector", vzc)
                            vnr = mk.op("vector", lambda e, h=h, p=p, s=s,
                                        pv=ppv, z=invzbc, af=attn_fm:
                                        e.tensor_mul(
                                            af[64 * h:64 * h + 64, p,
                                               512 * s:512 * s + 512],
                                            pv[h][0:64, :], z[:]), mk.s_dve)
                            g_ppv[h] = vnr
                            g_invzbc = vnr
            g_rot_pe = (mk.s_pe, mk.s_pe.v)
            g_aug_pe = (mk.s_pe, mk.s_pe.v)
            g_wqk[par] = (mk.s_pe, mk.s_pe.v)
            g_vh = (mk.s_pe, mk.s_pe.v)
        mk.flush()

    if "attn" in taps and seq == 0:
        mk.op("sync", lambda e: e.dma_start(tap_t["attn"], attn_fm[:]),
              mk.s_st, 16)
        mk.wait("sync", (mk.s_st, mk.s_st.v))
        mk.flush()
    if stop_after == "attn":
        return

    with ExitStack() as ss:
        awout_sb = ss.enter_context(sb(f"awo{seq}", [128, KC, D], BF16))
        xt = [ss.enter_context(sb(f"x3t{seq}_{i}", [128, D], F32))
              for i in range(2)]
        x2t = [ss.enter_context(sb(f"x2t{seq}_{i}", [128, D], F32))
               for i in range(2)]
        ln2t = [ss.enter_context(sb(f"ln2t{seq}_{i}", [128, D], BF16))
                for i in range(2)]
        stats = [ss.enter_context(sb(f"lnst2{seq}_{i}", [128, 8], F32))
                 for i in range(2)]
        sqdump = ss.enter_context(sb(f"sqd2{seq}", [128, D], BF16))
        po = [ss.enter_context(ps(f"po{seq}_{i}", [128, 512], F32))
              for i in range(4)]
        ptA = ss.enter_context(ps(f"pt2A{seq}", [128, 512], BF16))
        ptB = ss.enter_context(ps(f"pt2B{seq}", [128, 512], BF16))

        vaw = mk.op("gpsimd", lambda e, a=awout_sb: e.dma_start(
            a[:], dr["awout"]), mk.s_w, 16)
        mk.wait("tensor", vaw)
        pe_guard = {0: 0, 1: 0}
        g_po = [0, 0, 0, 0]
        buf_free = [0, 0]
        x2_free = [0, 0]
        ln_free = [0, 0]
        for i in range(T // 128):
            par = i % 2
            mk.wait("sync", buf_free[par])
            vx = mk.op("sync", lambda e, i=i, par=par, xt=xt, t0=tok0:
                       e.dma_start(xt[par][:],
                                   dr["x"][t0 + 128 * i:t0 + 128 * i + 128,
                                           :]), mk.s_x, 16)
            for half in range(2):
                dstp = po[2 * par + half]
                mk.wait("tensor", g_po[2 * par + half])
                for kc in range(KC):
                    mk.op("tensor", lambda e, i=i, kc=kc, half=half,
                          dstp=dstp, af=attn_fm, aw=awout_sb: e.matmul(
                              dstp[:], af[:, kc, 128 * i:128 * i + 128],
                              aw[:, kc, 512 * half:512 * half + 512],
                              start=(kc == 0), stop=(kc == KC - 1)),
                          mk.s_pe if kc == KC - 1 else None)
            vmm = (mk.s_pe, mk.s_pe.v)
            mk.wait("vector", vmm)
            mk.wait("vector", vx)
            mk.wait("vector", x2_free[par])
            for half in range(2):
                sl = slice(512 * half, 512 * half + 512)
                dstp = po[2 * par + half]
                mk.op("vector", lambda e, sl=sl, par=par, dstp=dstp, x2=x2t,
                      bf=bfull: e.tensor_add(x2[par][:, sl], dstp[:],
                                             bf[:, sl]), None)
            vadd = mk.op("vector", lambda e, par=par, x2=x2t, xt=xt:
                         e.tensor_add(x2[par][:], x2[par][:], xt[par][:]),
                         mk.s_dve)
            g_po[2 * par] = vadd
            g_po[2 * par + 1] = vadd
            buf_free[par] = vadd
            mk.wait("sync", vadd)
            vsp = mk.op("sync", lambda e, i=i, par=par, x2=x2t, t0=tok0:
                        e.dma_start(x2hbm[t0 + 128 * i:t0 + 128 * i + 128, :],
                                    x2[par][:]), mk.s_st, 16)
            x2_free[par] = vsp
            mk.wait("scalar", vadd)
            mk.wait("vector", vadd)
            mk.wait("scalar", ln_free[par])
            vi = _layernorm_tile(mk, x2t[par], ln2t[par], stats[par], sqdump)
            mk.wait("tensor", vi)
            vtr = _transpose_tile(mk, ln2t[par], ln2buf, i, ptA, ptB, ident,
                                  pe_guard)
            ln_free[par] = vtr
        if seq == 0:
            mk.wait("sync", ln_free[0])
            mk.wait("sync", ln_free[1])
            mk.op("sync", lambda e: e.dma_start(ln2hbm, ln2buf[:]),
                  mk.s_st, 16)
        mk.flush()

    if "ln2" in taps and seq == 0:
        mk.op("sync", lambda e: e.dma_start(tap_t["ln2"], ln2buf[:]),
              mk.s_st, 16)
        mk.wait("sync", (mk.s_st, mk.s_st.v))
        mk.flush()


def _ffn_phase(mk, es, sb, ps, dr, ln2buf, ln2hbm, g17_sb, c_tm, g_tm,
               x2hbm, y, tap_t, taps, stop_after):
    out_acc = es.enter_context(sb("out_acc", [128, 16, D], F32))
    lgt = es.enter_context(sb("lgt", [128, 17], F32))
    r_sc = es.enter_context(sb("r_sc", [128, 8], F32))
    e1t = es.enter_context(sb("e1t", [128, 16], F32))
    is1t = es.enter_context(sb("is1t", [128, 16], F32))
    l2t = es.enter_context(sb("l2t", [128, 16], F32))
    indt = es.enter_context(sb("indt", [128, 16], F32))
    sg = es.enter_context(sb("sg", [128, 512], F32))
    tsc = es.enter_context(sb("tsc", [128, 512], F32))
    sh_fm = es.enter_context(sb("sh_fm", [128, 8, T], BF16))
    h_e = es.enter_context(sb("h_e", [128, 4, T], BF16))
    x2t = [es.enter_context(sb(f"x2b{i}", [128, D], F32)) for i in range(2)]

    pA = es.enter_context(ps("pfA", [128, 512], F32))
    pB = es.enter_context(ps("pfB", [128, 512], F32))
    pCD = [es.enter_context(ps(f"pf{c}", [128, 512], F32))
           for c in ("C", "D", "E", "F")]

    SH_BLOCKS = ((0, 8), (8, 16), (16, 22))

    for half_idx, half in enumerate((1, 0)):
        tok0 = half * T
        til0 = half * 16
        if half_idx == 1:
            vld = mk.op("sync", lambda e: e.dma_start(ln2buf[:], ln2hbm),
                        mk.s_x, 16)
            mk.wait("tensor", vld)

        if "ln2ffn" in taps and half_idx == 0:
            mk.op("sync", lambda e: e.dma_start(tap_t["ln2ffn"], ln2buf[:]),
                  mk.s_st, 16)
            mk.wait("sync", (mk.s_st, mk.s_st.v))
        g_pA = 0
        g_rt_dve = 0
        g_rt_act = 0
        for i in range(16):
            mk.wait("tensor", g_pA)
            for kc in range(KC):
                mk.op("tensor", lambda e, i=i, kc=kc, lb=ln2buf, g=g17_sb,
                      d=pA: e.matmul(d[:, 0:17],
                                     lb[:, kc, 128 * i:128 * i + 128],
                                     g[:, kc, :], start=(kc == 0),
                                     stop=(kc == KC - 1)),
                      mk.s_pe if kc == KC - 1 else None)
            vr = (mk.s_pe, mk.s_pe.v)
            mk.wait("scalar", vr)
            mk.wait("scalar", g_rt_dve)
            g_pA = mk.op("scalar", lambda e, d=pA, l=lgt: e.copy(
                l[:], d[:, 0:17]), mk.s_act)
            mk.wait("vector", g_pA)
            mk.wait("vector", g_rt_act)
            v1 = mk.op("vector", lambda e, l=lgt, r=r_sc: e.reduce_max(
                r[:, 0:1], l[:, 0:16], AX.X), mk.s_dve)
            mk.wait("vector", v1)
            v2 = mk.op("vector", lambda e, r=r_sc: e.tensor_scalar_mul(
                r[:, 1:2], r[:, 0:1], -1.0), mk.s_dve)
            vi1 = mk.op("vector", lambda e, l=lgt, r=r_sc, o=is1t:
                        e.tensor_scalar(o[:], l[:, 0:16], r[:, 0:1], None,
                                        ALU.is_ge), mk.s_dve)
            mk.wait("vector", vi1)
            v3 = mk.op("vector", lambda e, l=lgt, o=is1t, d=l2t:
                       e.scalar_tensor_tensor(d[:], o[:], -1e30, l[:, 0:16],
                                              ALU.mult, ALU.add), mk.s_dve)
            mk.wait("vector", v3)
            v4 = mk.op("vector", lambda e, d=l2t, r=r_sc: e.reduce_max(
                r[:, 2:3], d[:], AX.X), mk.s_dve)
            mk.wait("scalar", v2)
            mk.wait("scalar", v4)
            v5 = mk.op("scalar", lambda e, l=lgt, r=r_sc, o=e1t:
                       e.activation(o[:], l[:, 0:16], AF.Exp,
                                    bias=r[:, 1:2]), mk.s_act)
            v6 = mk.op("scalar", lambda e, r=r_sc: e.activation(
                r[:, 3:4], r[:, 2:3], AF.Exp, bias=r[:, 1:2]), mk.s_act)
            vg = mk.op("scalar", lambda e, i=i, til0=til0, l=lgt, g=g_tm:
                       e.activation(g[:, til0 + i:til0 + i + 1], l[:, 16:17],
                                    AF.Sigmoid), mk.s_act)
            mk.wait("vector", vg)
            v7 = mk.op("vector", lambda e, r=r_sc: e.tensor_scalar_add(
                r[:, 3:4], r[:, 3:4], 1.0), mk.s_dve)
            mk.wait("vector", v7)
            v8 = mk.op("vector", lambda e, r=r_sc: e.reciprocal(
                r[:, 4:5], r[:, 3:4]), mk.s_dve)
            vi2 = mk.op("vector", lambda e, l=lgt, r=r_sc, o=indt:
                        e.tensor_scalar(o[:], l[:, 0:16], r[:, 2:3], None,
                                        ALU.is_ge), mk.s_dve)
            mk.wait("vector", v8)
            mk.wait("vector", vi2)
            v9 = mk.op("vector", lambda e, i=i, til0=til0, o=e1t, r=r_sc,
                       n=indt, c=c_tm: e.scalar_tensor_tensor(
                           c[:, til0 + i, :], o[:], r[:, 4:5], n[:],
                           ALU.mult, ALU.mult), mk.s_dve)
            mk.wait("vector", v9)
            g_rt_dve = v9
            g_rt_act = vg
        router_done = (mk.s_dve, mk.s_dve.v)

        if stop_after == "router":
            if half_idx == 0:
                mk.wait("sync", router_done)
                if "ctm" in taps:
                    mk.op("sync", lambda e: e.dma_start(tap_t["ctm"],
                                                        c_tm[:]), mk.s_st, 16)
                if "gtm" in taps:
                    mk.op("sync", lambda e: e.dma_start(tap_t["gtm"],
                                                        g_tm[:]), mk.s_st, 16)
                mk.wait("sync", (mk.s_st, mk.s_st.v))
                mk.flush()
            return

        shs = ExitStack()
        asg_sb = [shs.enter_context(sb(f"asg{half_idx}_{i}",
                                       [128, KC, 128], BF16))
                  for i in range(2)]
        asu_sb = [shs.enter_context(sb(f"asu{half_idx}_{i}",
                                       [128, KC, 128], BF16))
                  for i in range(2)]
        asd_sb = shs.enter_context(sb(f"asd_sb{half_idx}", [128, 8, D],
                                      BF16))
        g_asg = [0, 0]
        g_asd = 0
        g_pA_ = g_pA
        g_pB_ = 0
        g_pCD = [0, 0, 0, 0]
        g_sh_pe = 0
        g_sg = 0
        first_block = True
        for (mb0, mb1) in SH_BLOCKS:
            nmb = mb1 - mb0
            for mloc, m in enumerate(range(mb0, mb1)):
                par = m % 2
                mk.wait("gpsimd", g_asg[par])
                mk.op("gpsimd", lambda e, m=m, par=par, a=asg_sb:
                      e.dma_start(a[par][:], dr["asg"][m]), mk.s_w, 16)
                vwu = mk.op("gpsimd", lambda e, m=m, par=par, a=asu_sb:
                            e.dma_start(a[par][:], dr["asu"][m]), mk.s_w, 16)
                mk.wait("tensor", vwu)
                for n in range(4):
                    mk.wait("tensor", g_pA_)
                    mk.wait("tensor", g_pB_)
                    for kc in range(KC):
                        mk.op("tensor", lambda e, kc=kc, n=n, par=par,
                              a=asg_sb, lb=ln2buf, d=pA: e.matmul(
                                  d[:], a[par][:, kc, :],
                                  lb[:, kc, 512 * n:512 * n + 512],
                                  start=(kc == 0), stop=(kc == KC - 1)),
                              mk.s_pe if kc == KC - 1 else None)
                    vg_ = (mk.s_pe, mk.s_pe.v)
                    for kc in range(KC):
                        mk.op("tensor", lambda e, kc=kc, n=n, par=par,
                              a=asu_sb, lb=ln2buf, d=pB: e.matmul(
                                  d[:], a[par][:, kc, :],
                                  lb[:, kc, 512 * n:512 * n + 512],
                                  start=(kc == 0), stop=(kc == KC - 1)),
                              mk.s_pe if kc == KC - 1 else None)
                    vu_ = (mk.s_pe, mk.s_pe.v)
                    mk.wait("scalar", vg_)
                    mk.wait("scalar", g_sg)
                    vsg = mk.op("scalar", lambda e, d=pA, s=sg: e.activation(
                        s[:], d[:], AF.Silu), mk.s_act)
                    g_pA_ = vsg
                    mk.wait("vector", vu_)
                    mk.wait("vector", vsg)
                    if mloc == 0 and n == 0:
                        mk.wait("vector", g_sh_pe)
                    vh_ = mk.op("vector", lambda e, mloc=mloc, n=n, s=sg,
                                d=pB, sh=sh_fm: e.tensor_mul(
                                    sh[:, mloc, 512 * n:512 * n + 512],
                                    s[:], d[:]), mk.s_dve)
                    g_pB_ = vh_
                    g_sg = vh_
                g_asg[par] = (mk.s_pe, mk.s_pe.v)
            sh_done = (mk.s_dve, mk.s_dve.v)
            if "shfm" in taps and half_idx == 0 and mb0 == 0:
                mk.wait("sync", sh_done)
                mk.op("sync", lambda e, s=sh_fm: e.dma_start(
                    tap_t["shfm"], s[:]), mk.s_st, 16)
            mk.wait("gpsimd", g_asd)
            vwd = mk.op("gpsimd", lambda e, mb0=mb0, nmb=nmb, a=asd_sb:
                        e.dma_start(a[:, 0:nmb, :],
                                    dr["asd"][:, mb0:mb0 + nmb, :]),
                        mk.s_w, 16)
            mk.wait("tensor", vwd)
            mk.wait("tensor", sh_done)
            mk.wait("scalar", router_done)
            for i in range(16):
                pd = (pCD[0], pCD[1]) if i % 2 == 0 else (pCD[2], pCD[3])
                mk.wait("tensor", g_pCD[2 * (i % 2)])
                mk.wait("tensor", g_pCD[2 * (i % 2) + 1])
                for half_d in range(2):
                    for mloc in range(nmb):
                        mk.op("tensor", lambda e, i=i, mloc=mloc, nmb=nmb,
                              half_d=half_d, dstp=pd[half_d], sh=sh_fm,
                              a=asd_sb: e.matmul(
                                  dstp[:], sh[:, mloc, 128 * i:128 * i + 128],
                                  a[:, mloc, 512 * half_d:512 * half_d + 512],
                                  start=(mloc == 0), stop=(mloc == nmb - 1)),
                              mk.s_pe if mloc == nmb - 1 else None)
                vdn = (mk.s_pe, mk.s_pe.v)
                mk.wait("scalar", vdn)
                for half_d in range(2):
                    sl = slice(512 * half_d, 512 * half_d + 512)
                    vsc = mk.op("scalar", lambda e, i=i, til0=til0,
                                dstp=pd[half_d], t=tsc, g=g_tm:
                                e.activation(t[:], dstp[:], AF.Copy,
                                             scale=g[:,
                                                     til0 + i:til0 + i + 1]),
                                mk.s_act)
                    g_pCD[2 * (i % 2) + half_d] = vsc
                    mk.wait("vector", vsc)
                    if first_block:
                        vac = mk.op("vector", lambda e, i=i, sl=sl, t=tsc,
                                    o=out_acc: e.tensor_copy(
                                        o[:, i, sl], t[:]), mk.s_dve)
                    else:
                        vac = mk.op("vector", lambda e, i=i, sl=sl, t=tsc,
                                    o=out_acc: e.tensor_add(
                                        o[:, i, sl], o[:, i, sl], t[:]),
                                    mk.s_dve)
                    mk.wait("scalar", vac)
            if "gtm2" in taps and half_idx == 0 and mb0 == 0:
                mk.wait("sync", (mk.s_act, mk.s_act.v))
                mk.op("sync", lambda e: e.dma_start(tap_t["gtm2"], g_tm[:]),
                      mk.s_st, 16)
            if "outacc1" in taps and half_idx == 0 and mb0 == 0:
                mk.wait("sync", (mk.s_dve, mk.s_dve.v))
                mk.op("sync", lambda e: e.dma_start(tap_t["outacc1"],
                                                    out_acc[:]), mk.s_st, 16)
            first_block = False
            g_sh_pe = (mk.s_pe, mk.s_pe.v)
            g_asd = (mk.s_pe, mk.s_pe.v)
        mk.flush()
        shs.close()

        if stop_after == "shared":
            if half_idx == 0 and "outacc" in taps:
                mk.wait("sync", (mk.s_dve, mk.s_dve.v))
                mk.op("sync", lambda e: e.dma_start(tap_t["outacc"],
                                                    out_acc[:]), mk.s_st, 16)
                mk.wait("sync", (mk.s_st, mk.s_st.v))
                mk.flush()
            return

        mos = ExitStack()
        agu_sb = [mos.enter_context(sb(f"agu{half_idx}_{i}",
                                       [128, KC, 2, 128], BF16))
                  for i in range(2)]
        ad_sb = [mos.enter_context(sb(f"ad{half_idx}_{i}", [128, 4, D],
                                      BF16))
                 for i in range(2)]
        g_agu = [0, 0]
        g_ad = [0, 0]
        g_he = 0
        vda = None
        for e_i in range(NE):
            par = e_i % 2
            for mp in range(4):
                mk.wait("gpsimd", g_agu[par])
                vga = mk.op("gpsimd", lambda e, e_i=e_i, mp=mp, par=par,
                            a=agu_sb: e.dma_start(a[par][:],
                                                  dr["agu"][e_i, mp]),
                            mk.s_w, 16)
                if mp == 0:
                    mk.wait("gpsimd", g_ad[par])
                    vda = mk.op("gpsimd", lambda e, e_i=e_i, par=par,
                                a=ad_sb: e.dma_start(a[par][:],
                                                     dr["ad"][e_i]),
                                mk.s_w, 16)
                mk.wait("tensor", vga)
                for n in range(4):
                    mk.wait("tensor", g_pA_)
                    mk.wait("tensor", g_pB_)
                    for gu in range(2):
                        dstp = pA if gu == 0 else pB
                        for kc in range(KC):
                            mk.op("tensor", lambda e, kc=kc, n=n, gu=gu,
                                  par=par, dstp=dstp, a=agu_sb, lb=ln2buf:
                                  e.matmul(dstp[:], a[par][:, kc, gu, :],
                                           lb[:, kc, 512 * n:512 * n + 512],
                                           start=(kc == 0),
                                           stop=(kc == KC - 1)),
                                  mk.s_pe if kc == KC - 1 else None)
                        if gu == 0:
                            vg_ = (mk.s_pe, mk.s_pe.v)
                        else:
                            vu_ = (mk.s_pe, mk.s_pe.v)
                    mk.wait("scalar", vg_)
                    mk.wait("scalar", g_sg)
                    vsg = mk.op("scalar", lambda e, d=pA, s=sg: e.activation(
                        s[:], d[:], AF.Silu), mk.s_act)
                    g_pA_ = vsg
                    mk.wait("vector", vu_)
                    mk.wait("vector", vsg)
                    if mp == 0 and n == 0:
                        mk.wait("vector", g_he)
                    vh_ = mk.op("vector", lambda e, mp=mp, n=n, s=sg, d=pB,
                                he=h_e: e.tensor_mul(
                                    he[:, mp, 512 * n:512 * n + 512],
                                    s[:], d[:]), mk.s_dve)
                    g_pB_ = vh_
                    g_sg = vh_
                g_agu[par] = (mk.s_pe, mk.s_pe.v)
            he_done = (mk.s_dve, mk.s_dve.v)
            mk.wait("tensor", vda)
            mk.wait("tensor", he_done)
            for i in range(16):
                pd = (pCD[0], pCD[1]) if i % 2 == 0 else (pCD[2], pCD[3])
                mk.wait("tensor", g_pCD[2 * (i % 2)])
                mk.wait("tensor", g_pCD[2 * (i % 2) + 1])
                for half_d in range(2):
                    for kc in range(4):
                        mk.op("tensor", lambda e, i=i, kc=kc, half_d=half_d,
                              par=par, dstp=pd[half_d], he=h_e, a=ad_sb:
                              e.matmul(dstp[:],
                                       he[:, kc, 128 * i:128 * i + 128],
                                       a[par][:, kc,
                                              512 * half_d:512 * half_d + 512],
                                       start=(kc == 0), stop=(kc == 3)),
                              mk.s_pe if kc == 3 else None)
                vdn = (mk.s_pe, mk.s_pe.v)
                mk.wait("scalar", vdn)
                for half_d in range(2):
                    sl = slice(512 * half_d, 512 * half_d + 512)
                    vsc = mk.op("scalar", lambda e, i=i, til0=til0, e_i=e_i,
                                dstp=pd[half_d], t=tsc, c=c_tm:
                                e.activation(t[:], dstp[:], AF.Copy,
                                             scale=c[:, til0 + i,
                                                     e_i:e_i + 1]),
                                mk.s_act)
                    g_pCD[2 * (i % 2) + half_d] = vsc
                    mk.wait("vector", vsc)
                    vac = mk.op("vector", lambda e, i=i, sl=sl, t=tsc,
                                o=out_acc: e.tensor_add(o[:, i, sl],
                                                        o[:, i, sl], t[:]),
                                mk.s_dve)
                    mk.wait("scalar", vac)
            g_ad[par] = (mk.s_pe, mk.s_pe.v)
            g_he = (mk.s_pe, mk.s_pe.v)
        moe_done = (mk.s_dve, mk.s_dve.v)
        mk.flush()
        mos.close()

        if "outacc" in taps and half_idx == 0:
            mk.wait("sync", moe_done)
            mk.op("sync", lambda e: e.dma_start(tap_t["outacc"], out_acc[:]),
                  mk.s_st, 16)

        g_x2 = [0, 0]
        g_out = [0, 0]
        for i in range(16):
            par = i % 2
            mk.wait("sync", g_x2[par])
            vx = mk.op("sync", lambda e, i=i, par=par, x2=x2t, t0=tok0:
                       e.dma_start(x2[par][:],
                                   x2hbm[t0 + 128 * i:t0 + 128 * i + 128, :]),
                       mk.s_x, 16)
            mk.wait("vector", vx)
            mk.wait("vector", g_out[par])
            mk.wait("vector", moe_done)
            vad = mk.op("vector", lambda e, i=i, par=par, o=out_acc, x2=x2t:
                        e.tensor_add(x2[par][:], o[:, i, :], x2[par][:]),
                        mk.s_dve)
            g_x2[par] = vad
            mk.wait("sync", vad)
            vst = mk.op("sync", lambda e, i=i, par=par, x2=x2t, t0=tok0:
                        e.dma_start(y[t0 + 128 * i:t0 + 128 * i + 128, :],
                                    x2[par][:]), mk.s_st, 16)
            g_out[par] = vst
            g_x2[par] = vst
        mk.flush()


_NC_CACHE = {}


def _get_nc():
    if "nc" not in _NC_CACHE:
        _NC_CACHE["nc"] = build_nc()
    return _NC_CACHE["nc"]


def kernel(**inputs):
    w = prep_weights(inputs)
    x = np.ascontiguousarray(np.asarray(inputs["L0_attn_x"],
                                        dtype=np.float32))
    nc = _get_nc()
    in_maps = []
    for c in range(NCORES):
        m = {"x": np.ascontiguousarray(x[2 * c:2 * c + 2].reshape(NTOK, D))}
        m.update(w)
        in_maps.append(m)
    res = run_bass_kernel_spmd(nc, in_maps, core_ids=list(range(NCORES)))
    out = np.empty((B, T, D), dtype=np.float32)
    for c in range(NCORES):
        out[2 * c:2 * c + 2] = res.results[c]["y"].reshape(2, T, D)
    return out


# version 02
# speedup vs baseline: 1.1010x; 1.1010x over previous
import numpy as np
import ml_dtypes
from contextlib import ExitStack

import concourse.bass as bass
import concourse.mybir as mybir
from concourse.bass_utils import run_bass_kernel_spmd

F32 = mybir.dt.float32
BF16 = mybir.dt.bfloat16
AF = mybir.ActivationFunctionType
ALU = mybir.AluOpType
AX = mybir.AxisListType

B, T, D, H, Dh = 16, 2048, 1024, 16, 64
NCORES = 8
SEQS = 2
NTOK = SEQS * T
NP = H // 2
KC = D // 128
SH_M = 22
NE = 16
EPS = 1e-5

_bf16 = ml_dtypes.bfloat16



def _perm_eo(block64):
    return np.concatenate([block64[0::2], block64[1::2]], axis=0)


def prep_weights(inp):
    w_qkv = np.asarray(inp["L0_attn_w_qkv"], dtype=np.float32)
    b_qkv = np.asarray(inp["L0_attn_b_qkv"], dtype=np.float32)
    w_out = np.asarray(inp["L0_attn_w_out"], dtype=np.float32)
    b_out = np.asarray(inp["L0_attn_b_out"], dtype=np.float32)
    gate_w = np.asarray(inp["L0_ffn_gate_w"], dtype=np.float32)
    w_gu = np.asarray(inp["L0_ffn_w_gate_up"], dtype=np.float32)
    w_dn = np.asarray(inp["L0_ffn_w_down"], dtype=np.float32)
    swg = np.asarray(inp["L0_ffn_shared_w_gate"], dtype=np.float32)
    swu = np.asarray(inp["L0_ffn_shared_w_up"], dtype=np.float32)
    swd = np.asarray(inp["L0_ffn_shared_w_down"], dtype=np.float32)
    sgw = np.asarray(inp["L0_ffn_shared_gate_w"], dtype=np.float32)

    wq, wk, wv = w_qkv[0:D], w_qkv[D:2 * D], w_qkv[2 * D:3 * D]
    bq, bk, bv = b_qkv[0:D], b_qkv[D:2 * D], b_qkv[2 * D:3 * D]

    aqk = np.empty((NP, 128, KC, 2, 128), dtype=np.float32)
    bqk = np.empty((128, 2 * NP), dtype=np.float32)
    for p in range(NP):
        a, b_ = 2 * p, 2 * p + 1
        for t_i, (w, bias) in enumerate(((wq, bq), (wk, bk))):
            ea = _perm_eo(w[64 * a:64 * a + 64])
            eb = _perm_eo(w[64 * b_:64 * b_ + 64])
            rows = np.concatenate([ea[0:32], eb[0:32], ea[32:64], eb[32:64]],
                                  axis=0)
            ba = _perm_eo(bias[64 * a:64 * a + 64])
            bb = _perm_eo(bias[64 * b_:64 * b_ + 64])
            bqk[:, 2 * p + t_i] = np.concatenate(
                [ba[0:32], bb[0:32], ba[32:64], bb[32:64]], axis=0)
            for kc in range(KC):
                aqk[p, :, kc, t_i, :] = rows[:, 128 * kc:128 * kc + 128].T

    av = np.empty((NP, 128, KC, 128), dtype=np.float32)
    for p in range(NP):
        cols = wv[128 * p:128 * p + 128]
        for kc in range(KC):
            av[p, :, kc, :] = cols[:, 128 * kc:128 * kc + 128].T
    bvh = np.empty((128, KC), dtype=np.float32)
    for kc in range(KC):
        bvh[:, kc] = bv[128 * kc:128 * kc + 128]

    awout = np.empty((128, KC, D), dtype=np.float32)
    for kc in range(KC):
        awout[:, kc, :] = w_out[:, 128 * kc:128 * kc + 128].T

    g17 = np.empty((128, KC, 17), dtype=np.float32)
    for kc in range(KC):
        g17[:, kc, 0:16] = gate_w[:, 128 * kc:128 * kc + 128].T
        g17[:, kc, 16] = sgw[0, 128 * kc:128 * kc + 128]

    agu = np.empty((NE, 4, 128, KC, 2, 128), dtype=np.float32)
    for e in range(NE):
        for mp in range(4):
            for kc in range(KC):
                agu[e, mp, :, kc, 0, :] = w_gu[e, 128 * kc:128 * kc + 128,
                                               128 * mp:128 * mp + 128]
                agu[e, mp, :, kc, 1, :] = w_gu[e, 128 * kc:128 * kc + 128,
                                               512 + 128 * mp:
                                               512 + 128 * mp + 128]

    ad = np.ascontiguousarray(np.transpose(
        w_dn.reshape(NE, 4, 128, D), (0, 2, 1, 3)))

    asg = np.empty((SH_M, 128, KC, 128), dtype=np.float32)
    asu = np.empty((SH_M, 128, KC, 128), dtype=np.float32)
    for m in range(SH_M):
        for kc in range(KC):
            asg[m, :, kc, :] = swg[128 * m:128 * m + 128,
                                   128 * kc:128 * kc + 128].T
            asu[m, :, kc, :] = swu[128 * m:128 * m + 128,
                                   128 * kc:128 * kc + 128].T
    asd = np.ascontiguousarray(np.transpose(swd.reshape(D, SH_M, 128),
                                            (2, 1, 0)))

    invf = 10000.0 ** (-(np.arange(0, Dh, 2, dtype=np.float64) / Dh))
    fr = invf[:, None] * np.arange(T, dtype=np.float64)[None, :]
    c32 = np.cos(fr).astype(np.float32)
    s32 = np.sin(fr).astype(np.float32)
    css = np.ascontiguousarray(np.concatenate([c32, c32, s32, s32], axis=0))
    ssc = np.ascontiguousarray(np.concatenate([s32, s32, c32, c32], axis=0))

    return {
        "aqk": np.ascontiguousarray(aqk), "bqk": bqk,
        "av": np.ascontiguousarray(av), "bvh": bvh,
        "awout": np.ascontiguousarray(awout),
        "bout": np.ascontiguousarray(b_out[None, :]),
        "g17": np.ascontiguousarray(g17),
        "agu": np.ascontiguousarray(agu), "ad": ad,
        "asg": np.ascontiguousarray(asg), "asu": np.ascontiguousarray(asu),
        "asd": asd, "css": css, "ssc": ssc,
    }



ENGINES = ("tensor", "vector", "scalar", "gpsimd", "sync")


class Sem:
    def __init__(self, nc, name):
        self.h = nc.alloc_semaphore(name=name)
        self.v = 0

    def inc(self, n=1):
        self.v += n
        return self.v


class MK:
    def __init__(self, nc):
        self.nc = nc
        self.ops = {e: [] for e in ENGINES}
        self.s_w = Sem(nc, "s_wdma")
        self.s_x = Sem(nc, "s_xdma")
        self.s_st = Sem(nc, "s_store")
        self.s_pe = Sem(nc, "s_pe")
        self.s_act = Sem(nc, "s_act")
        self.s_dve = Sem(nc, "s_dve")

    def op(self, eng, fn, sem=None, inc=1):
        if sem is None:
            self.ops[eng].append(fn)
            return None
        v = sem.inc(inc)
        self.ops[eng].append(lambda e, fn=fn, s=sem.h, i=inc:
                             fn(e).then_inc(s, i))
        return (sem, v)

    def wait(self, eng, g):
        if not g:
            return
        sem, v = g
        if v <= 0:
            return
        self.ops[eng].append(lambda e, s=sem.h, v=v: e.wait_ge(s, v))

    def flush(self):
        ops = self.ops
        self.ops = {e: [] for e in ENGINES}

        def runner(lst):
            def f(eng):
                for o in lst:
                    o(eng)
            return f
        with self.nc.Block() as blk:
            for eng in ENGINES:
                if ops[eng]:
                    getattr(blk, eng)(runner(ops[eng]))


def _layernorm_tile(mk, src, dst_bf16, stats, sqdump):
    va = mk.op("scalar", lambda e, s=src, st=stats, d=sqdump: e.activation(
        d[:], s[:], AF.Square, accum_out=st[:, 1:2]), mk.s_act)
    vb = mk.op("vector", lambda e, s=src, st=stats: e.reduce_sum(
        st[:, 0:1], s[:], AX.X), mk.s_dve)
    mk.wait("vector", va)
    mk.op("vector", lambda e, st=stats: e.tensor_scalar_mul(
        st[:, 2:3], st[:, 0:1], 1.0 / D), None)
    vc = mk.op("vector", lambda e, st=stats: e.tensor_scalar_mul(
        st[:, 3:4], st[:, 1:2], 1.0 / D), mk.s_dve)
    mk.wait("vector", vc)
    vd = mk.op("vector", lambda e, st=stats: e.tensor_mul(
        st[:, 4:5], st[:, 2:3], st[:, 2:3]), mk.s_dve)
    mk.wait("vector", vd)
    ve = mk.op("vector", lambda e, st=stats: e.tensor_scalar(
        st[:, 5:6], st[:, 3:4], st[:, 4:5], float(EPS), ALU.subtract,
        ALU.add), mk.s_dve)
    mk.wait("scalar", ve)
    vf = mk.op("scalar", lambda e, st=stats: e.activation(
        st[:, 5:6], st[:, 5:6], AF.Sqrt), mk.s_act)
    mk.wait("vector", vf)
    vg = mk.op("vector", lambda e, st=stats: e.reciprocal(
        st[:, 6:7], st[:, 5:6]), mk.s_dve)
    mk.wait("vector", vg)
    vh = mk.op("vector", lambda e, st=stats: e.tensor_scalar(
        st[:, 7:8], st[:, 2:3], st[:, 6:7], -1.0, ALU.mult, ALU.mult),
        mk.s_dve)
    mk.wait("scalar", vh)
    vi = mk.op("scalar", lambda e, s=src, d=dst_bf16, st=stats: e.activation(
        d[:], s[:], AF.Identity, bias=st[:, 7:8], scale=st[:, 6:7]), mk.s_act)
    return vi


def _transpose_tile(mk, ln_tm, fm_dst, i, ptA, ptB, ident, pe_guard):
    last = None
    for g, pt in ((0, ptA), (1, ptB)):
        mk.wait("tensor", pe_guard[g])
        for c4 in range(4):
            c = 4 * g + c4
            mk.op("tensor", lambda e, c=c, c4=c4, pt=pt, ln=ln_tm, idn=ident:
                  e.transpose(pt[:, 128 * c4:128 * c4 + 128],
                              ln[:, 128 * c:128 * c + 128], idn[:]),
                  mk.s_pe if c4 == 3 else None)
        vpe = (mk.s_pe, mk.s_pe.v)
        mk.wait("scalar", vpe)
        v = mk.op("scalar", lambda e, g=g, pt=pt, i=i, d=fm_dst: e.copy(
            d[:, 4 * g:4 * g + 4, 128 * i:128 * i + 128], pt[:]), mk.s_act)
        pe_guard[g] = v
        last = v
    return last


def build_nc(taps=(), stop_after=None):
    nc = bass.Bass("TRN2", target_bir_lowering=False)
    taps = set(taps)

    dt_in = [
        ("x", [NTOK, D], F32), ("aqk", [NP, 128, KC, 2, 128], F32),
        ("bqk", [128, 2 * NP], F32), ("av", [NP, 128, KC, 128], F32),
        ("bvh", [128, KC], F32), ("awout", [128, KC, D], F32),
        ("bout", [1, D], F32), ("g17", [128, KC, 17], F32),
        ("agu", [NE, 4, 128, KC, 2, 128], F32), ("ad", [NE, 128, 4, D], F32),
        ("asg", [SH_M, 128, KC, 128], F32), ("asu", [SH_M, 128, KC, 128], F32),
        ("asd", [128, SH_M, D], F32), ("css", [128, T], F32),
        ("ssc", [128, T], F32),
    ]
    dr = {n: nc.dram_tensor(n, s, d, kind="ExternalInput").ap()
          for n, s, d in dt_in}
    y = nc.dram_tensor("y", [NTOK, D], F32, kind="ExternalOutput").ap()
    x2hbm = nc.dram_tensor("x2hbm", [NTOK, D], F32, kind="Internal").ap()
    ln2hbm = nc.dram_tensor("ln2hbm", [128, KC, T], BF16, kind="Internal").ap()

    tap_t = {}
    tap_specs = {
        "ln1": ("t_ln1", [128, KC, T], BF16),
        "qrot": ("t_qrot", [128, T], BF16), "krot": ("t_krot", [128, T], BF16),
        "attn": ("t_attn", [128, KC, T], BF16),
        "ln2": ("t_ln2", [128, KC, T], BF16),
        "ctm": ("t_ctm", [128, 32, 16], F32), "gtm": ("t_gtm", [128, 32], F32),
        "outacc": ("t_outacc", [128, 16, D], F32),
        "vtap": ("t_vtap", [128, 16, 65], BF16),
        "x2": ("t_x2", [NTOK, D], F32),
        "shfm": ("t_shfm", [128, 8, T], BF16),
        "outacc1": ("t_outacc1", [128, 16, D], F32),
        "ln2ffn": ("t_ln2ffn", [128, KC, T], BF16),
        "gtm2": ("t_gtm2", [128, 32], F32),
        "dbg1": ("t_dbg1", [128, 1], F32),
        "dbg2": ("t_dbg2", [128, 512], BF16),
        "dbg3": ("t_dbg3", [1, 512], F32),
        "dbg4": ("t_dbg4", [32, 8], F32),
        "dbg5": ("t_dbg5", [2, T], F32),
    }
    for k in taps:
        name, shape, dt = tap_specs[k]
        tap_t[k] = nc.dram_tensor(name, shape, dt, kind="ExternalOutput").ap()

    ident_d = nc.inline_tensor(np.eye(128, dtype=_bf16), name="identc").ap()
    onesrow_d = nc.inline_tensor(np.ones((1, 128), dtype=_bf16),
                                 name="onesrowc").ap()
    o2 = np.zeros((128, 2), dtype=_bf16)
    o2[0:64, 0] = 1
    o2[64:128, 1] = 1
    ones2_d = nc.inline_tensor(o2, name="ones2c").ap()
    identf_d = nc.inline_tensor(np.eye(128, dtype=np.float32),
                                name="identfc").ap()
    _r = np.arange(128)
    m4q_d = nc.inline_tensor(
        np.where(_r[None, :] > _r[:, None], -1e9, 0.0).astype(np.float32),
        name="m4qc").ap()
    m4k_d = nc.inline_tensor(
        np.where(_r[:, None] > _r[None, :], -1e9, 0.0).astype(np.float32),
        name="m4kc").ap()

    mk = MK(nc)
    sb = nc.sbuf_tensor
    ps = nc.psum_tensor
    with ExitStack() as es:
        ident = es.enter_context(sb("ident", [128, 128], BF16))
        onesrow = es.enter_context(sb("onesrow", [1, 128], BF16))
        ones2 = es.enter_context(sb("ones2", [128, 2], BF16))
        bqk_sb = es.enter_context(sb("bqk_sb", [128, 2 * NP], F32))
        bfull = es.enter_context(sb("bfull", [128, D], F32))
        g17_sb = es.enter_context(sb("g17_sb", [128, KC, 17], BF16))
        c_tm = es.enter_context(sb("c_tm", [128, 32, 16], F32))
        g_tm = es.enter_context(sb("g_tm", [128, 32], F32))
        ln2buf = es.enter_context(sb("ln2buf", [128, KC, T], BF16))

        with ExitStack() as ss:
            awout_sb = ss.enter_context(sb("awout0", [128, KC, D], BF16))
            bvh_sb = ss.enter_context(sb("bvh_sb", [128, KC], F32))
            bvb = ss.enter_context(sb("bvb", [128, KC], BF16))
            bout_sb = ss.enter_context(sb("bout_sb", [1, D], F32))
            brow = ss.enter_context(sb("brow", [1, D], F32))
            browb = ss.enter_context(sb("browb", [1, D], BF16))
            pb = ss.enter_context(ps("pb", [128, 512], F32))
            pb2 = ss.enter_context(ps("pb2", [128, 512], F32))

            mk.op("sync", lambda e: e.dma_start(ident[:], ident_d), mk.s_x, 16)
            vor = mk.op("sync", lambda e: e.dma_start(onesrow[:], onesrow_d),
                        mk.s_x, 16)
            mk.op("sync", lambda e: e.dma_start(ones2[:], ones2_d),
                  mk.s_x, 16)
            mk.op("sync", lambda e: e.dma_start(bqk_sb[:], dr["bqk"]),
                  mk.s_x, 16)
            mk.op("gpsimd", lambda e: e.dma_start(g17_sb[:], dr["g17"]),
                  mk.s_w, 16)
            va = mk.op("gpsimd", lambda e: e.dma_start(awout_sb[:],
                                                       dr["awout"]),
                       mk.s_w, 16)
            vb = mk.op("sync", lambda e: e.dma_start(bvh_sb[:], dr["bvh"]),
                       mk.s_x, 16)
            vo = mk.op("sync", lambda e: e.dma_start(bout_sb[:], dr["bout"]),
                       mk.s_x, 16)
            mk.wait("scalar", vb)
            vc = mk.op("scalar", lambda e: e.copy(bvb[:], bvh_sb[:]),
                       mk.s_act)
            mk.wait("tensor", vc)
            mk.wait("tensor", va)
            for half in range(2):
                dst = pb if half == 0 else pb2
                for kc in range(KC):
                    mk.op("tensor", lambda e, kc=kc, h=half, d=dst: e.matmul(
                        d[0:1, :], bvb[:, kc:kc + 1],
                        awout_sb[:, kc, 512 * h:512 * h + 512],
                        start=(kc == 0), stop=(kc == KC - 1)),
                        mk.s_pe if kc == KC - 1 else None)
            vpe = (mk.s_pe, mk.s_pe.v)
            mk.wait("vector", vpe)
            mk.wait("vector", vo)
            mk.op("vector", lambda e: e.tensor_add(
                brow[0:1, 0:512], pb[0:1, :], bout_sb[0:1, 0:512]), None)
            vd = mk.op("vector", lambda e: e.tensor_add(
                brow[0:1, 512:1024], pb2[0:1, :], bout_sb[0:1, 512:1024]),
                mk.s_dve)
            mk.wait("scalar", vd)
            vc2 = mk.op("scalar", lambda e: e.copy(browb[:], brow[:]),
                        mk.s_act)
            mk.wait("tensor", vc2)
            mk.wait("tensor", vor)
            mk.wait("tensor", vd)
            for half in range(2):
                dst = pb if half == 0 else pb2
                mk.op("tensor", lambda e, h=half, d=dst: e.matmul(
                    d[:], onesrow[0:1, :], browb[0:1, 512 * h:512 * h + 512],
                    start=True, stop=True), mk.s_pe)
            vpe2 = (mk.s_pe, mk.s_pe.v)
            mk.wait("scalar", vpe2)
            mk.op("scalar", lambda e: e.copy(bfull[:, 0:512], pb[:]),
                  mk.s_act)
            mk.op("scalar", lambda e: e.copy(bfull[:, 512:1024], pb2[:]),
                  mk.s_act)
            mk.flush()

        with ExitStack() as att:
            css = att.enter_context(sb("css_sb", [128, T], F32))
            ssc = att.enter_context(sb("ssc_sb", [128, T], F32))
            identf = att.enter_context(sb("identf", [128, 128], F32))
            m4q_sb = att.enter_context(sb("m4q_sb", [128, 128], F32))
            m4k_sb = att.enter_context(sb("m4k_sb", [128, 128], F32))
            ln1_fm = att.enter_context(sb("ln1_fm", [128, KC, T], BF16))
            attn_fm = att.enter_context(sb("attn_fm", [128, KC, T], BF16))

            mk.op("sync", lambda e: e.dma_start(css[:], dr["css"]),
                  mk.s_x, 16)
            mk.op("sync", lambda e: e.dma_start(ssc[:], dr["ssc"]),
                  mk.s_x, 16)
            mk.op("sync", lambda e: e.dma_start(identf[:], identf_d),
                  mk.s_x, 16)
            mk.op("sync", lambda e: e.dma_start(m4q_sb[:], m4q_d),
                  mk.s_x, 16)
            mk.op("sync", lambda e: e.dma_start(m4k_sb[:], m4k_d),
                  mk.s_x, 16)
            mk.flush()

            for seq in range(SEQS):
                _attn_seq(mk, sb, ps, dr, seq, css, ssc, m4q_sb, m4k_sb,
                          identf, ln1_fm, attn_fm, ln2buf, bqk_sb, bfull,
                          ident, onesrow, x2hbm, ln2hbm, tap_t, taps,
                          stop_after)
                if stop_after in ("ln1", "attn") and seq == 0:
                    break

        if stop_after in ("ln1", "attn", "a3"):
            if "x2" in taps and stop_after == "a3":
                mk.op("sync", lambda e: e.dma_start(tap_t["x2"], x2hbm),
                      mk.s_st, 16)
            mk.wait("sync", (mk.s_st, mk.s_st.v))
            mk.flush()
            return nc

        with ExitStack() as fes:
            _ffn_phase(mk, fes, sb, ps, dr, ln2buf, ln2hbm, g17_sb, c_tm,
                       g_tm, x2hbm, y, tap_t, taps, stop_after)

        if "x2" in taps:
            mk.op("sync", lambda e: e.dma_start(tap_t["x2"], x2hbm),
                  mk.s_st, 16)
        mk.wait("sync", (mk.s_st, mk.s_st.v))
        mk.flush()
    return nc


def _attn_seq(mk, sb, ps, dr, seq, css, ssc, m4q_sb, m4k_sb, identf,
              ln1_fm, attn_fm, ln2buf, bqk_sb, bfull, ident, onesrow,
              x2hbm, ln2hbm, tap_t, taps, stop_after):
    tok0 = seq * T

    with ExitStack() as ss:
        xt = [ss.enter_context(sb(f"xt{seq}_{i}", [128, D], F32))
              for i in range(2)]
        lnt = [ss.enter_context(sb(f"lnt{seq}_{i}", [128, D], BF16))
               for i in range(2)]
        stats = [ss.enter_context(sb(f"lnstat{seq}_{i}", [128, 8], F32))
                 for i in range(2)]
        sqdump = ss.enter_context(sb(f"sqd{seq}", [128, D], BF16))
        ptA = ss.enter_context(ps(f"ptA{seq}", [128, 512], BF16))
        ptB = ss.enter_context(ps(f"ptB{seq}", [128, 512], BF16))
        pe_guard = {0: 0, 1: 0}
        buf_free = [0, 0]
        ln_free = [0, 0]
        for i in range(T // 128):
            par = i % 2
            mk.wait("sync", buf_free[par])
            vx = mk.op("sync", lambda e, i=i, par=par, xt=xt, t0=tok0:
                       e.dma_start(xt[par][:],
                                   dr["x"][t0 + 128 * i:t0 + 128 * i + 128,
                                           :]), mk.s_x, 16)
            mk.wait("scalar", vx)
            mk.wait("vector", vx)
            mk.wait("scalar", ln_free[par])
            vi = _layernorm_tile(mk, xt[par], lnt[par], stats[par], sqdump)
            buf_free[par] = vi
            mk.wait("tensor", vi)
            vtr = _transpose_tile(mk, lnt[par], ln1_fm, i, ptA, ptB, ident,
                                  pe_guard)
            ln_free[par] = vtr
        mk.flush()

    if "ln1" in taps and seq == 0:
        mk.op("sync", lambda e: e.dma_start(tap_t["ln1"], ln1_fm[:]),
              mk.s_st, 16)
        mk.wait("sync", (mk.s_st, mk.s_st.v))
        mk.flush()
    if stop_after == "ln1":
        return
    with ExitStack() as ss:
        wqk = [ss.enter_context(sb(f"wqk{seq}_{i}", [128, KC, 2, 128], BF16))
               for i in range(2)]
        avp = [ss.enter_context(sb(f"avp{seq}_{i}", [128, KC, 128], BF16))
               for i in range(2)]
        qst = ss.enter_context(sb(f"qst{seq}", [128, 512], F32))
        kst = ss.enter_context(sb(f"kst{seq}", [128, 512], F32))
        mt1 = ss.enter_context(sb(f"mt1{seq}", [128, 512], F32))
        mt2 = ss.enter_context(sb(f"mt2{seq}", [128, 512], F32))
        mo = ss.enter_context(sb(f"mo{seq}", [64, 512], F32))
        me = ss.enter_context(sb(f"me{seq}", [64, 512], F32))
        qrot = ss.enter_context(sb(f"qrot{seq}", [128, T], BF16))
        krot = ss.enter_context(sb(f"krot{seq}", [128, T], BF16))
        qag = [ss.enter_context(sb(f"qag{seq}_{h}", [65, T], BF16))
               for h in range(2)]
        kag = [ss.enter_context(sb(f"kag{seq}_{h}", [65, T], BF16))
               for h in range(2)]
        macc = ss.enter_context(sb(f"macc{seq}", [128, 4], F32))
        cmx = ss.enter_context(sb(f"cmx{seq}", [128, 1], F32))
        vh = [ss.enter_context(sb(f"vh{seq}_{h}", [128, 16, 65], BF16))
              for h in range(2)]
        prb = [ss.enter_context(sb(f"prb{seq}_{hj}", [128, 512], BF16))
               for hj in range(4)]
        invzr = ss.enter_context(sb(f"invzr{seq}", [1, 512], F32))
        invzb = ss.enter_context(sb(f"invzb{seq}", [1, 512], BF16))
        invzbc = ss.enter_context(sb(f"invzbc{seq}", [64, 512], F32))
        pq = ss.enter_context(ps(f"pq{seq}", [128, 512], F32))
        pk = ss.enter_context(ps(f"pk{seq}", [128, 512], F32))
        psA = [ss.enter_context(ps(f"psA{seq}_{j}", [128, 512], F32))
               for j in range(2)]
        psB = [ss.enter_context(ps(f"psB{seq}_{j}", [128, 512], F32))
               for j in range(2)]
        ppv = [ss.enter_context(ps(f"ppv{seq}_{h}", [65, 512], F32))
               for h in range(2)]

        mk.op("vector", lambda e, v=vh: e.memset(v[0][:, :, 64:65], 1.0),
              None)
        mk.op("vector", lambda e, v=vh: e.memset(v[1][:, :, 64:65], 1.0),
              None)
        mk.op("vector", lambda e, k=kag: e.memset(k[0][64:65, :], 1.0), None)
        mk.op("vector", lambda e, k=kag: e.memset(k[1][64:65, :], 1.0),
              mk.s_dve)

        g_wqk = [0, 0]
        g_avp = [0, 0]
        g_pq = 0
        g_pk = 0
        g_psA = [0, 0]
        g_psB = [0, 0]
        g_ppv = [0, 0]
        g_prb = [0, 0, 0, 0]
        g_rot_pe = 0
        g_aug_pe = 0
        g_vh = 0
        g_stage = 0
        g_invzr = 0
        g_invzbc = 0
        g_macc = 0

        for p in range(NP):
            par = p % 2
            mk.wait("gpsimd", g_wqk[par])
            vw = mk.op("gpsimd", lambda e, p=p, par=par, w=wqk: e.dma_start(
                w[par][:], dr["aqk"][p]), mk.s_w, 16)
            mk.wait("gpsimd", g_avp[par])
            vav = mk.op("gpsimd", lambda e, p=p, par=par, a=avp: e.dma_start(
                a[par][:], dr["av"][p]), mk.s_w, 16)

            mk.wait("tensor", vw)
            for n in range(4):
                for t_i in range(2):
                    dst = pq if t_i == 0 else pk
                    mk.wait("tensor", g_pq if t_i == 0 else g_pk)
                    for kc in range(KC):
                        mk.op("tensor", lambda e, n=n, kc=kc, t_i=t_i,
                              par=par, dst=dst, w=wqk, lf=ln1_fm: e.matmul(
                                  dst[:], w[par][:, kc, t_i, :],
                                  lf[:, kc, 512 * n:512 * n + 512],
                                  start=(kc == 0), stop=(kc == KC - 1)),
                              mk.s_pe if kc == KC - 1 else None)
                    vmm = (mk.s_pe, mk.s_pe.v)
                    mk.wait("scalar", vmm)
                    mk.wait("scalar", g_stage)
                    dstf = qst if t_i == 0 else kst
                    bia = 2 * p + t_i
                    vev = mk.op("scalar", lambda e, dstf=dstf, bia=bia,
                                dst=dst, bq=bqk_sb: e.activation(
                                    dstf[:], dst[:],
                                    AF.Identity, bias=bq[:, bia:bia + 1]),
                                mk.s_act)
                    if t_i == 0:
                        g_pq = vev
                    else:
                        g_pk = vev
                mk.wait("vector", g_pk)
                if n == 0:
                    mk.wait("vector", g_rot_pe)
                for srcf, dstrot in ((qst, qrot), (kst, krot)):
                    sl = slice(512 * n, 512 * n + 512)
                    mk.op("vector", lambda e, s=srcf, sl=sl, m=mt1, c=css:
                          e.tensor_mul(m[:], s[:], c[:, sl]), None)
                    v1 = mk.op("vector", lambda e, s=srcf, sl=sl, m=mt2,
                               c=ssc: e.tensor_mul(m[:], s[:], c[:, sl]),
                               mk.s_dve)
                    mk.wait("vector", v1)
                    mk.op("vector", lambda e, m=mt1, o=mo: e.tensor_copy(
                        o[:], m[64:128, :]), None)
                    v2 = mk.op("vector", lambda e, m=mt2, o=me: e.tensor_copy(
                        o[:], m[64:128, :]), mk.s_dve)
                    mk.wait("vector", v2)
                    mk.op("vector", lambda e, d=dstrot, sl=sl, m=mt1, o=mo:
                          e.tensor_sub(d[0:32, sl], m[0:32, :], o[0:32, :]),
                          None)
                    mk.op("vector", lambda e, d=dstrot, sl=sl, m=mt2, o=me:
                          e.tensor_add(d[32:64, sl], m[0:32, :], o[0:32, :]),
                          None)
                    mk.op("vector", lambda e, d=dstrot, sl=sl, m=mt1, o=mo:
                          e.tensor_sub(d[64:96, sl], m[32:64, :],
                                       o[32:64, :]), None)
                    v3 = mk.op("vector", lambda e, d=dstrot, sl=sl, m=mt2,
                               o=me: e.tensor_add(d[96:128, sl], m[32:64, :],
                                                  o[32:64, :]), mk.s_dve)
                    mk.wait("vector", v3)
                g_stage = (mk.s_dve, mk.s_dve.v)
            rope_done = (mk.s_dve, mk.s_dve.v)

            mk.wait("vector", g_aug_pe)
            mk.op("vector", lambda e, q=qag, r=qrot: e.tensor_copy(
                q[0][0:64, :], r[0:64, :]), None)
            mk.op("vector", lambda e, q=qag, r=qrot: e.tensor_copy(
                q[1][0:64, :], r[64:128, :]), None)
            mk.op("vector", lambda e, k=kag, r=krot: e.tensor_copy(
                k[0][0:64, :], r[0:64, :]), None)
            vcp = mk.op("vector", lambda e, k=kag, r=krot: e.tensor_copy(
                k[1][0:64, :], r[64:128, :]), mk.s_dve)
            aug_rows_done = vcp

            mk.wait("tensor", vav)
            mk.wait("vector", g_vh)
            for i in range(T // 128):
                dstp = psB[i % 2]
                mk.wait("tensor", g_psB[i % 2])
                for kc in range(KC):
                    mk.op("tensor", lambda e, i=i, kc=kc, par=par, dstp=dstp,
                          a=avp, lf=ln1_fm: e.matmul(
                              dstp[:, 0:128],
                              lf[:, kc, 128 * i:128 * i + 128],
                              a[par][:, kc, :],
                              start=(kc == 0), stop=(kc == KC - 1)),
                          mk.s_pe if kc == KC - 1 else None)
                vv = (mk.s_pe, mk.s_pe.v)
                mk.wait("vector", vv)
                mk.op("vector", lambda e, i=i, dstp=dstp, v=vh:
                      e.tensor_copy(v[0][:, i, 0:64], dstp[:, 0:64]), None)
                g_psB[i % 2] = mk.op(
                    "vector", lambda e, i=i, dstp=dstp, v=vh: e.tensor_copy(
                        v[1][:, i, 0:64], dstp[:, 64:128]), mk.s_dve)
            g_avp[par] = (mk.s_pe, mk.s_pe.v)
            v_done = (mk.s_dve, mk.s_dve.v)

            if seq == 0 and p == 0:
                if "vtap" in taps:
                    mk.wait("sync", v_done)
                    mk.op("sync", lambda e, v=vh: e.dma_start(tap_t["vtap"],
                                                              v[0][:]),
                          mk.s_st, 16)
                if "qrot" in taps:
                    mk.wait("sync", rope_done)
                    mk.op("sync", lambda e: e.dma_start(tap_t["qrot"],
                                                        qrot[:]), mk.s_st, 16)
                    mk.op("sync", lambda e: e.dma_start(tap_t["krot"],
                                                        krot[:]), mk.s_st, 16)

            if p == 0 and seq == 0:
                mk.wait("vector", (mk.s_x, mk.s_x.v))
            for i in range(T // 128):
                nk = i // 4 + 1
                u_d = i % 4
                for h in range(2):
                    mk.wait("vector", g_macc)
                    for n in range(nk):
                        jp = n % 2
                        pst = psA if h == 0 else psB
                        mk.wait("tensor", g_psA[jp] if h == 0 else
                                g_psB[jp])
                        if i == 0 and n == 0:
                            mk.wait("tensor", rope_done)
                        tp = (0, 0) if h == 0 else (64, 0)
                        r0 = 64 * h
                        mk.op("tensor", lambda e, i=i, n=n, r0=r0, tp=tp,
                              dstp=pst[jp], qr=qrot, kr=krot: e.matmul(
                                  dstp[:], qr[r0:r0 + 64,
                                              128 * i:128 * i + 128],
                                  kr[r0:r0 + 64, 512 * n:512 * n + 512],
                                  start=True, stop=True, tile_position=tp),
                              mk.s_pe)
                        vqk = (mk.s_pe, mk.s_pe.v)
                        mk.wait("vector", vqk)
                        if n == i // 4:
                            vmk = mk.op("vector", lambda e, u=u_d,
                                        dstp=pst[jp], m=m4q_sb:
                                        e.tensor_add(
                                            dstp[:, 128 * u:128 * u + 128],
                                            dstp[:, 128 * u:128 * u + 128],
                                            m[:]), mk.s_dve)
                            mk.wait("vector", vmk)
                            red_sl = slice(0, 128 * (u_d + 1))
                        else:
                            red_sl = slice(0, 512)
                        if n == 0:
                            vred = mk.op("vector", lambda e, h=h, sl=red_sl,
                                         dstp=pst[jp], mc=macc: e.reduce_max(
                                             mc[:, h:h + 1], dstp[:, sl],
                                             AX.X), mk.s_dve)
                        else:
                            vred = mk.op("vector", lambda e, sl=red_sl,
                                         dstp=pst[jp], c=cmx: e.reduce_max(
                                             c[:], dstp[:, sl], AX.X),
                                         mk.s_dve)
                            mk.wait("vector", vred)
                            vred = mk.op("vector", lambda e, h=h, mc=macc,
                                         c=cmx: e.tensor_max(
                                             mc[:, h:h + 1], mc[:, h:h + 1],
                                             c[:]), mk.s_dve)
                        if h == 0:
                            g_psA[jp] = vred
                        else:
                            g_psB[jp] = vred
                        mk.wait("vector", vred)
                vng = mk.op("vector", lambda e, mc=macc: e.tensor_scalar_mul(
                    mc[:, 2:4], mc[:, 0:2], -1.0), mk.s_dve)
                mk.wait("tensor", vng)
                mk.wait("tensor", g_pk)
                for h in range(2):
                    mk.op("tensor", lambda e, h=h, mc=macc, d=pk, idf=identf,
                          i=i: e.transpose(d[0:1, 128 * h:128 * h + 128],
                                           mc[:, 2 + h:3 + h], idf[:]),
                          mk.s_pe)
                vtr = (mk.s_pe, mk.s_pe.v)
                g_macc = vtr
                mk.wait("scalar", vtr)
                mk.op("scalar", lambda e, i=i, q=qag, d=pk: e.copy(
                    q[0][64:65, 128 * i:128 * i + 128], d[0:1, 0:128]), None)
                vev = mk.op("scalar", lambda e, i=i, q=qag, d=pk: e.copy(
                    q[1][64:65, 128 * i:128 * i + 128], d[0:1, 128:256]),
                    mk.s_act)
                g_pk = vev
            aug_done = (mk.s_act, mk.s_act.v)

            for s in range(4):
                nj = 4 * s + 4
                mk.wait("tensor", g_ppv[0])
                mk.wait("tensor", g_ppv[1])
                for j in range(nj):
                    jp = j % 2
                    diag_r = j - 4 * s
                    for h in range(2):
                        pst = psA if h == 0 else psB
                        mk.wait("tensor", g_psA[jp] if h == 0 else
                                g_psB[jp])
                        if j == 0 and h == 0 and s == 0:
                            mk.wait("tensor", aug_done)
                            mk.wait("tensor", aug_rows_done)
                            mk.wait("tensor", (mk.s_dve,
                                               mk.s_dve.v))
                        mk.op("tensor", lambda e, j=j, s=s, h=h,
                              dstp=pst[jp], ka=kag, qa=qag: e.matmul(
                                  dstp[:], ka[h][:, 128 * j:128 * j + 128],
                                  qa[h][:, 512 * s:512 * s + 512],
                                  start=True, stop=True), mk.s_pe)
                        vqk = (mk.s_pe, mk.s_pe.v)
                        pbuf = prb[2 * h + jp]
                        mk.wait("scalar", vqk)
                        mk.wait("scalar", g_prb[2 * h + jp])
                        if diag_r >= 0:
                            mk.wait("vector", vqk)
                            mk.wait("vector", g_prb[2 * h + jp])
                            vdg = mk.op("vector", lambda e, r=diag_r,
                                        dstp=pst[jp], m=m4k_sb:
                                        e.tensor_add(
                                            dstp[:, 128 * r:128 * r + 128],
                                            dstp[:, 128 * r:128 * r + 128],
                                            m[:]), mk.s_dve)
                            if diag_r > 0:
                                vz = mk.op("vector", lambda e, r=diag_r,
                                           pbuf=pbuf: e.memset(
                                               pbuf[:, 0:128 * r], 0.0),
                                           mk.s_dve)
                            mk.wait("scalar", vdg)
                            if diag_r > 0:
                                mk.wait("scalar", vz)
                            sl_e = slice(128 * diag_r, 512)
                        else:
                            sl_e = slice(0, 512)
                        vex = mk.op("scalar", lambda e, pbuf=pbuf, sl=sl_e,
                                    dstp=pst[jp]: e.activation(
                                        pbuf[:, sl], dstp[:, sl], AF.Exp,
                                        scale=0.125), mk.s_act)
                        if h == 0:
                            g_psA[jp] = vex
                        else:
                            g_psB[jp] = vex
                        prb_ready = vex
                        if (seq == 0 and p == 0 and s == 0 and j == 0
                                and h == 0 and "dbg2" in taps):
                            mk.wait("sync", prb_ready)
                            mk.op("sync", lambda e, pbuf=pbuf: e.dma_start(
                                tap_t["dbg2"], pbuf[:]), mk.s_st, 16)
                        mk.wait("tensor", prb_ready)
                        mk.op("tensor", lambda e, h=h, j=j, nj=nj, pbuf=pbuf,
                              v=vh, pv=ppv: e.matmul(
                                  pv[h][:], v[h][:, j, :], pbuf[:],
                                  start=(j == 0), stop=(j == nj - 1)),
                              mk.s_pe)
                        g_prb[2 * h + jp] = (mk.s_pe, mk.s_pe.v)
                        if j == nj - 1:
                            vpv_done = (mk.s_pe, mk.s_pe.v)
                            mk.wait("scalar", vpv_done)
                            mk.wait("scalar", g_invzr)
                            vln = mk.op("scalar", lambda e, h=h, pv=ppv,
                                        iz=invzr: e.activation(
                                            iz[0:1, :], pv[h][64:65, :],
                                            AF.Ln), mk.s_act)
                            mk.wait("scalar", vln)
                            vcb = mk.op("scalar", lambda e, iz=invzr,
                                        ib=invzb: e.activation(
                                            ib[:], iz[0:1, :], AF.Exp,
                                            scale=-1.0), mk.s_act)
                            g_invzr = vcb
                            if (seq == 0 and p == 0 and s == 0 and h == 0
                                    and "dbg3" in taps):
                                mk.wait("sync", vcb)
                                mk.op("sync", lambda e: e.dma_start(
                                    tap_t["dbg3"], invzr[:]), mk.s_st, 16)
                            mk.wait("tensor", vcb)
                            mk.wait("tensor", g_pq)
                            mk.op("tensor", lambda e, o=onesrow, ib=invzb,
                                  d=pq: e.matmul(d[0:64, :], o[0:1, 0:64],
                                                 ib[0:1, :], start=True,
                                                 stop=True), mk.s_pe)
                            vbc = (mk.s_pe, mk.s_pe.v)
                            mk.wait("scalar", vbc)
                            mk.wait("scalar", g_invzbc)
                            vzc = mk.op("scalar", lambda e, d=pq, z=invzbc:
                                        e.copy(z[:], d[0:64, :]), mk.s_act)
                            g_pq = vzc
                            mk.wait("vector", vzc)
                            vnr = mk.op("vector", lambda e, h=h, p=p, s=s,
                                        pv=ppv, z=invzbc, af=attn_fm:
                                        e.tensor_mul(
                                            af[64 * h:64 * h + 64, p,
                                               512 * s:512 * s + 512],
                                            pv[h][0:64, :], z[:]), mk.s_dve)
                            g_ppv[h] = vnr
                            g_invzbc = vnr
            g_rot_pe = (mk.s_pe, mk.s_pe.v)
            g_aug_pe = (mk.s_pe, mk.s_pe.v)
            g_wqk[par] = (mk.s_pe, mk.s_pe.v)
            g_vh = (mk.s_pe, mk.s_pe.v)
        mk.flush()

    if "attn" in taps and seq == 0:
        mk.op("sync", lambda e: e.dma_start(tap_t["attn"], attn_fm[:]),
              mk.s_st, 16)
        mk.wait("sync", (mk.s_st, mk.s_st.v))
        mk.flush()
    if stop_after == "attn":
        return

    with ExitStack() as ss:
        awout_sb = ss.enter_context(sb(f"awo{seq}", [128, KC, D], BF16))
        xt = [ss.enter_context(sb(f"x3t{seq}_{i}", [128, D], F32))
              for i in range(2)]
        x2t = [ss.enter_context(sb(f"x2t{seq}_{i}", [128, D], F32))
               for i in range(2)]
        ln2t = [ss.enter_context(sb(f"ln2t{seq}_{i}", [128, D], BF16))
                for i in range(2)]
        stats = [ss.enter_context(sb(f"lnst2{seq}_{i}", [128, 8], F32))
                 for i in range(2)]
        sqdump = ss.enter_context(sb(f"sqd2{seq}", [128, D], BF16))
        po = [ss.enter_context(ps(f"po{seq}_{i}", [128, 512], F32))
              for i in range(4)]
        ptA = ss.enter_context(ps(f"pt2A{seq}", [128, 512], BF16))
        ptB = ss.enter_context(ps(f"pt2B{seq}", [128, 512], BF16))

        vaw = mk.op("gpsimd", lambda e, a=awout_sb: e.dma_start(
            a[:], dr["awout"]), mk.s_w, 16)
        mk.wait("tensor", vaw)
        pe_guard = {0: 0, 1: 0}
        g_po = [0, 0, 0, 0]
        buf_free = [0, 0]
        x2_free = [0, 0]
        ln_free = [0, 0]
        for i in range(T // 128):
            par = i % 2
            mk.wait("sync", buf_free[par])
            vx = mk.op("sync", lambda e, i=i, par=par, xt=xt, t0=tok0:
                       e.dma_start(xt[par][:],
                                   dr["x"][t0 + 128 * i:t0 + 128 * i + 128,
                                           :]), mk.s_x, 16)
            for half in range(2):
                dstp = po[2 * par + half]
                mk.wait("tensor", g_po[2 * par + half])
                for kc in range(KC):
                    mk.op("tensor", lambda e, i=i, kc=kc, half=half,
                          dstp=dstp, af=attn_fm, aw=awout_sb: e.matmul(
                              dstp[:], af[:, kc, 128 * i:128 * i + 128],
                              aw[:, kc, 512 * half:512 * half + 512],
                              start=(kc == 0), stop=(kc == KC - 1)),
                          mk.s_pe if kc == KC - 1 else None)
            vmm = (mk.s_pe, mk.s_pe.v)
            mk.wait("vector", vmm)
            mk.wait("vector", vx)
            mk.wait("vector", x2_free[par])
            for half in range(2):
                sl = slice(512 * half, 512 * half + 512)
                dstp = po[2 * par + half]
                mk.op("vector", lambda e, sl=sl, par=par, dstp=dstp, x2=x2t,
                      bf=bfull: e.tensor_add(x2[par][:, sl], dstp[:],
                                             bf[:, sl]), None)
            vadd = mk.op("vector", lambda e, par=par, x2=x2t, xt=xt:
                         e.tensor_add(x2[par][:], x2[par][:], xt[par][:]),
                         mk.s_dve)
            g_po[2 * par] = vadd
            g_po[2 * par + 1] = vadd
            buf_free[par] = vadd
            mk.wait("sync", vadd)
            vsp = mk.op("sync", lambda e, i=i, par=par, x2=x2t, t0=tok0:
                        e.dma_start(x2hbm[t0 + 128 * i:t0 + 128 * i + 128, :],
                                    x2[par][:]), mk.s_st, 16)
            x2_free[par] = vsp
            mk.wait("scalar", vadd)
            mk.wait("vector", vadd)
            mk.wait("scalar", ln_free[par])
            vi = _layernorm_tile(mk, x2t[par], ln2t[par], stats[par], sqdump)
            mk.wait("tensor", vi)
            vtr = _transpose_tile(mk, ln2t[par], ln2buf, i, ptA, ptB, ident,
                                  pe_guard)
            ln_free[par] = vtr
        if seq == 0:
            mk.wait("sync", ln_free[0])
            mk.wait("sync", ln_free[1])
            mk.op("sync", lambda e: e.dma_start(ln2hbm, ln2buf[:]),
                  mk.s_st, 16)
        mk.flush()

    if "ln2" in taps and seq == 0:
        mk.op("sync", lambda e: e.dma_start(tap_t["ln2"], ln2buf[:]),
              mk.s_st, 16)
        mk.wait("sync", (mk.s_st, mk.s_st.v))
        mk.flush()


def _ffn_phase(mk, es, sb, ps, dr, ln2buf, ln2hbm, g17_sb, c_tm, g_tm,
               x2hbm, y, tap_t, taps, stop_after):
    out_acc = es.enter_context(sb("out_acc", [128, 16, D], F32))
    lgt = es.enter_context(sb("lgt", [128, 17], F32))
    r_sc = es.enter_context(sb("r_sc", [128, 8], F32))
    e1t = es.enter_context(sb("e1t", [128, 16], F32))
    is1t = es.enter_context(sb("is1t", [128, 16], F32))
    l2t = es.enter_context(sb("l2t", [128, 16], F32))
    indt = es.enter_context(sb("indt", [128, 16], F32))
    sg = es.enter_context(sb("sg", [128, 512], F32))
    tsc = es.enter_context(sb("tsc", [128, 512], F32))
    sh_fm = es.enter_context(sb("sh_fm", [128, 8, T], BF16))
    h_e = es.enter_context(sb("h_e", [128, 4, T], BF16))
    x2t = [es.enter_context(sb(f"x2b{i}", [128, D], F32)) for i in range(2)]

    pA = es.enter_context(ps("pfA", [128, 512], F32))
    pB = es.enter_context(ps("pfB", [128, 512], F32))
    pCD = [es.enter_context(ps(f"pf{c}", [128, 512], F32))
           for c in ("C", "D", "E", "F")]

    SH_BLOCKS = ((0, 8), (8, 16), (16, 22))

    for half_idx, half in enumerate((1, 0)):
        tok0 = half * T
        til0 = half * 16
        if half_idx == 1:
            vld = mk.op("sync", lambda e: e.dma_start(ln2buf[:], ln2hbm),
                        mk.s_x, 16)
            mk.wait("tensor", vld)

        if "ln2ffn" in taps and half_idx == 0:
            mk.op("sync", lambda e: e.dma_start(tap_t["ln2ffn"], ln2buf[:]),
                  mk.s_st, 16)
            mk.wait("sync", (mk.s_st, mk.s_st.v))
        g_pA = 0
        g_rt_dve = 0
        g_rt_act = 0
        for i in range(16):
            mk.wait("tensor", g_pA)
            for kc in range(KC):
                mk.op("tensor", lambda e, i=i, kc=kc, lb=ln2buf, g=g17_sb,
                      d=pA: e.matmul(d[:, 0:17],
                                     lb[:, kc, 128 * i:128 * i + 128],
                                     g[:, kc, :], start=(kc == 0),
                                     stop=(kc == KC - 1)),
                      mk.s_pe if kc == KC - 1 else None)
            vr = (mk.s_pe, mk.s_pe.v)
            mk.wait("scalar", vr)
            mk.wait("scalar", g_rt_dve)
            g_pA = mk.op("scalar", lambda e, d=pA, l=lgt: e.copy(
                l[:], d[:, 0:17]), mk.s_act)
            mk.wait("vector", g_pA)
            mk.wait("vector", g_rt_act)
            v1 = mk.op("vector", lambda e, l=lgt, r=r_sc: e.reduce_max(
                r[:, 0:1], l[:, 0:16], AX.X), mk.s_dve)
            mk.wait("vector", v1)
            v2 = mk.op("vector", lambda e, r=r_sc: e.tensor_scalar_mul(
                r[:, 1:2], r[:, 0:1], -1.0), mk.s_dve)
            vi1 = mk.op("vector", lambda e, l=lgt, r=r_sc, o=is1t:
                        e.tensor_scalar(o[:], l[:, 0:16], r[:, 0:1], None,
                                        ALU.is_ge), mk.s_dve)
            mk.wait("vector", vi1)
            v3 = mk.op("vector", lambda e, l=lgt, o=is1t, d=l2t:
                       e.scalar_tensor_tensor(d[:], o[:], -1e30, l[:, 0:16],
                                              ALU.mult, ALU.add), mk.s_dve)
            mk.wait("vector", v3)
            v4 = mk.op("vector", lambda e, d=l2t, r=r_sc: e.reduce_max(
                r[:, 2:3], d[:], AX.X), mk.s_dve)
            mk.wait("scalar", v2)
            mk.wait("scalar", v4)
            v5 = mk.op("scalar", lambda e, l=lgt, r=r_sc, o=e1t:
                       e.activation(o[:], l[:, 0:16], AF.Exp,
                                    bias=r[:, 1:2]), mk.s_act)
            v6 = mk.op("scalar", lambda e, r=r_sc: e.activation(
                r[:, 3:4], r[:, 2:3], AF.Exp, bias=r[:, 1:2]), mk.s_act)
            vg = mk.op("scalar", lambda e, i=i, til0=til0, l=lgt, g=g_tm:
                       e.activation(g[:, til0 + i:til0 + i + 1], l[:, 16:17],
                                    AF.Sigmoid), mk.s_act)
            mk.wait("vector", vg)
            v7 = mk.op("vector", lambda e, r=r_sc: e.tensor_scalar_add(
                r[:, 3:4], r[:, 3:4], 1.0), mk.s_dve)
            mk.wait("vector", v7)
            v8 = mk.op("vector", lambda e, r=r_sc: e.reciprocal(
                r[:, 4:5], r[:, 3:4]), mk.s_dve)
            vi2 = mk.op("vector", lambda e, l=lgt, r=r_sc, o=indt:
                        e.tensor_scalar(o[:], l[:, 0:16], r[:, 2:3], None,
                                        ALU.is_ge), mk.s_dve)
            mk.wait("vector", v8)
            mk.wait("vector", vi2)
            v9 = mk.op("vector", lambda e, i=i, til0=til0, o=e1t, r=r_sc,
                       n=indt, c=c_tm: e.scalar_tensor_tensor(
                           c[:, til0 + i, :], o[:], r[:, 4:5], n[:],
                           ALU.mult, ALU.mult), mk.s_dve)
            mk.wait("vector", v9)
            g_rt_dve = v9
            g_rt_act = vg
        router_done = (mk.s_dve, mk.s_dve.v)

        if stop_after == "router":
            if half_idx == 0:
                mk.wait("sync", router_done)
                if "ctm" in taps:
                    mk.op("sync", lambda e: e.dma_start(tap_t["ctm"],
                                                        c_tm[:]), mk.s_st, 16)
                if "gtm" in taps:
                    mk.op("sync", lambda e: e.dma_start(tap_t["gtm"],
                                                        g_tm[:]), mk.s_st, 16)
                mk.wait("sync", (mk.s_st, mk.s_st.v))
                mk.flush()
            return

        shs = ExitStack()
        asg_sb = [shs.enter_context(sb(f"asg{half_idx}_{i}",
                                       [128, KC, 128], BF16))
                  for i in range(2)]
        asu_sb = [shs.enter_context(sb(f"asu{half_idx}_{i}",
                                       [128, KC, 128], BF16))
                  for i in range(2)]
        asd_sb = shs.enter_context(sb(f"asd_sb{half_idx}", [128, 8, D],
                                      BF16))
        g_asg = [0, 0]
        g_asd = 0
        g_pA_ = g_pA
        g_pB_ = 0
        g_pCD = [0, 0, 0, 0]
        g_sh_pe = 0
        g_sg = 0
        first_block = True
        for (mb0, mb1) in SH_BLOCKS:
            nmb = mb1 - mb0
            for mloc, m in enumerate(range(mb0, mb1)):
                par = m % 2
                mk.wait("gpsimd", g_asg[par])
                mk.op("gpsimd", lambda e, m=m, par=par, a=asg_sb:
                      e.dma_start(a[par][:], dr["asg"][m]), mk.s_w, 16)
                vwu = mk.op("gpsimd", lambda e, m=m, par=par, a=asu_sb:
                            e.dma_start(a[par][:], dr["asu"][m]), mk.s_w, 16)
                mk.wait("tensor", vwu)
                for n in range(4):
                    mk.wait("tensor", g_pA_)
                    mk.wait("tensor", g_pB_)
                    for kc in range(KC):
                        mk.op("tensor", lambda e, kc=kc, n=n, par=par,
                              a=asg_sb, lb=ln2buf, d=pA: e.matmul(
                                  d[:], a[par][:, kc, :],
                                  lb[:, kc, 512 * n:512 * n + 512],
                                  start=(kc == 0), stop=(kc == KC - 1)),
                              mk.s_pe if kc == KC - 1 else None)
                    vg_ = (mk.s_pe, mk.s_pe.v)
                    for kc in range(KC):
                        mk.op("tensor", lambda e, kc=kc, n=n, par=par,
                              a=asu_sb, lb=ln2buf, d=pB: e.matmul(
                                  d[:], a[par][:, kc, :],
                                  lb[:, kc, 512 * n:512 * n + 512],
                                  start=(kc == 0), stop=(kc == KC - 1)),
                              mk.s_pe if kc == KC - 1 else None)
                    vu_ = (mk.s_pe, mk.s_pe.v)
                    mk.wait("scalar", vg_)
                    mk.wait("scalar", g_sg)
                    vsg = mk.op("scalar", lambda e, d=pA, s=sg: e.activation(
                        s[:], d[:], AF.Silu), mk.s_act)
                    g_pA_ = vsg
                    mk.wait("vector", vu_)
                    mk.wait("vector", vsg)
                    if mloc == 0 and n == 0:
                        mk.wait("vector", g_sh_pe)
                    vh_ = mk.op("vector", lambda e, mloc=mloc, n=n, s=sg,
                                d=pB, sh=sh_fm: e.tensor_mul(
                                    sh[:, mloc, 512 * n:512 * n + 512],
                                    s[:], d[:]), mk.s_dve)
                    g_pB_ = vh_
                    g_sg = vh_
                g_asg[par] = (mk.s_pe, mk.s_pe.v)
            sh_done = (mk.s_dve, mk.s_dve.v)
            if "shfm" in taps and half_idx == 0 and mb0 == 0:
                mk.wait("sync", sh_done)
                mk.op("sync", lambda e, s=sh_fm: e.dma_start(
                    tap_t["shfm"], s[:]), mk.s_st, 16)
            mk.wait("gpsimd", g_asd)
            vwd = mk.op("gpsimd", lambda e, mb0=mb0, nmb=nmb, a=asd_sb:
                        e.dma_start(a[:, 0:nmb, :],
                                    dr["asd"][:, mb0:mb0 + nmb, :]),
                        mk.s_w, 16)
            mk.wait("tensor", vwd)
            mk.wait("tensor", sh_done)
            mk.wait("scalar", router_done)
            for i in range(16):
                pd = (pCD[0], pCD[1]) if i % 2 == 0 else (pCD[2], pCD[3])
                mk.wait("tensor", g_pCD[2 * (i % 2)])
                mk.wait("tensor", g_pCD[2 * (i % 2) + 1])
                for half_d in range(2):
                    for mloc in range(nmb):
                        mk.op("tensor", lambda e, i=i, mloc=mloc, nmb=nmb,
                              half_d=half_d, dstp=pd[half_d], sh=sh_fm,
                              a=asd_sb: e.matmul(
                                  dstp[:], sh[:, mloc, 128 * i:128 * i + 128],
                                  a[:, mloc, 512 * half_d:512 * half_d + 512],
                                  start=(mloc == 0), stop=(mloc == nmb - 1)),
                              mk.s_pe if mloc == nmb - 1 else None)
                vdn = (mk.s_pe, mk.s_pe.v)
                mk.wait("vector", vdn)
                mk.wait("vector", router_done)
                for half_d in range(2):
                    sl = slice(512 * half_d, 512 * half_d + 512)
                    if first_block:
                        vac = mk.op("vector", lambda e, i=i, sl=sl, til0=til0,
                                    dstp=pd[half_d], o=out_acc, g=g_tm:
                                    e.tensor_scalar(
                                        o[:, i, sl], dstp[:],
                                        g[:, til0 + i:til0 + i + 1], None,
                                        ALU.mult), mk.s_dve)
                    else:
                        vac = mk.op("vector", lambda e, i=i, sl=sl, til0=til0,
                                    dstp=pd[half_d], o=out_acc, g=g_tm:
                                    e.scalar_tensor_tensor(
                                        o[:, i, sl], dstp[:],
                                        g[:, til0 + i:til0 + i + 1],
                                        o[:, i, sl], ALU.mult, ALU.add),
                                    mk.s_dve)
                    g_pCD[2 * (i % 2) + half_d] = vac
                    mk.wait("vector", vac)
            if "gtm2" in taps and half_idx == 0 and mb0 == 0:
                mk.wait("sync", (mk.s_act, mk.s_act.v))
                mk.op("sync", lambda e: e.dma_start(tap_t["gtm2"], g_tm[:]),
                      mk.s_st, 16)
            if "outacc1" in taps and half_idx == 0 and mb0 == 0:
                mk.wait("sync", (mk.s_dve, mk.s_dve.v))
                mk.op("sync", lambda e: e.dma_start(tap_t["outacc1"],
                                                    out_acc[:]), mk.s_st, 16)
            first_block = False
            g_sh_pe = (mk.s_pe, mk.s_pe.v)
            g_asd = (mk.s_pe, mk.s_pe.v)
        mk.flush()
        shs.close()

        if stop_after == "shared":
            if half_idx == 0 and "outacc" in taps:
                mk.wait("sync", (mk.s_dve, mk.s_dve.v))
                mk.op("sync", lambda e: e.dma_start(tap_t["outacc"],
                                                    out_acc[:]), mk.s_st, 16)
                mk.wait("sync", (mk.s_st, mk.s_st.v))
                mk.flush()
            return

        mos = ExitStack()
        agu_sb = [mos.enter_context(sb(f"agu{half_idx}_{i}",
                                       [128, KC, 2, 128], BF16))
                  for i in range(2)]
        ad_sb = [mos.enter_context(sb(f"ad{half_idx}_{i}", [128, 4, D],
                                      BF16))
                 for i in range(2)]
        g_agu = [0, 0]
        g_ad = [0, 0]
        g_he = 0
        vda = None
        for e_i in range(NE):
            par = e_i % 2
            for mp in range(4):
                mk.wait("gpsimd", g_agu[par])
                vga = mk.op("gpsimd", lambda e, e_i=e_i, mp=mp, par=par,
                            a=agu_sb: e.dma_start(a[par][:],
                                                  dr["agu"][e_i, mp]),
                            mk.s_w, 16)
                if mp == 0:
                    mk.wait("gpsimd", g_ad[par])
                    vda = mk.op("gpsimd", lambda e, e_i=e_i, par=par,
                                a=ad_sb: e.dma_start(a[par][:],
                                                     dr["ad"][e_i]),
                                mk.s_w, 16)
                mk.wait("tensor", vga)
                for n in range(4):
                    mk.wait("tensor", g_pA_)
                    mk.wait("tensor", g_pB_)
                    for gu in range(2):
                        dstp = pA if gu == 0 else pB
                        for kc in range(KC):
                            mk.op("tensor", lambda e, kc=kc, n=n, gu=gu,
                                  par=par, dstp=dstp, a=agu_sb, lb=ln2buf:
                                  e.matmul(dstp[:], a[par][:, kc, gu, :],
                                           lb[:, kc, 512 * n:512 * n + 512],
                                           start=(kc == 0),
                                           stop=(kc == KC - 1)),
                                  mk.s_pe if kc == KC - 1 else None)
                        if gu == 0:
                            vg_ = (mk.s_pe, mk.s_pe.v)
                        else:
                            vu_ = (mk.s_pe, mk.s_pe.v)
                    mk.wait("scalar", vg_)
                    mk.wait("scalar", g_sg)
                    vsg = mk.op("scalar", lambda e, d=pA, s=sg: e.activation(
                        s[:], d[:], AF.Silu), mk.s_act)
                    g_pA_ = vsg
                    mk.wait("vector", vu_)
                    mk.wait("vector", vsg)
                    if mp == 0 and n == 0:
                        mk.wait("vector", g_he)
                    vh_ = mk.op("vector", lambda e, mp=mp, n=n, s=sg, d=pB,
                                he=h_e: e.tensor_mul(
                                    he[:, mp, 512 * n:512 * n + 512],
                                    s[:], d[:]), mk.s_dve)
                    g_pB_ = vh_
                    g_sg = vh_
                g_agu[par] = (mk.s_pe, mk.s_pe.v)
            he_done = (mk.s_dve, mk.s_dve.v)
            mk.wait("tensor", vda)
            mk.wait("tensor", he_done)
            for i in range(16):
                pd = (pCD[0], pCD[1]) if i % 2 == 0 else (pCD[2], pCD[3])
                mk.wait("tensor", g_pCD[2 * (i % 2)])
                mk.wait("tensor", g_pCD[2 * (i % 2) + 1])
                for half_d in range(2):
                    for kc in range(4):
                        mk.op("tensor", lambda e, i=i, kc=kc, half_d=half_d,
                              par=par, dstp=pd[half_d], he=h_e, a=ad_sb:
                              e.matmul(dstp[:],
                                       he[:, kc, 128 * i:128 * i + 128],
                                       a[par][:, kc,
                                              512 * half_d:512 * half_d + 512],
                                       start=(kc == 0), stop=(kc == 3)),
                              mk.s_pe if kc == 3 else None)
                vdn = (mk.s_pe, mk.s_pe.v)
                mk.wait("vector", vdn)
                for half_d in range(2):
                    sl = slice(512 * half_d, 512 * half_d + 512)
                    vac = mk.op("vector", lambda e, i=i, sl=sl, til0=til0,
                                e_i=e_i, dstp=pd[half_d], o=out_acc, c=c_tm:
                                e.scalar_tensor_tensor(
                                    o[:, i, sl], dstp[:],
                                    c[:, til0 + i, e_i:e_i + 1],
                                    o[:, i, sl], ALU.mult, ALU.add),
                                mk.s_dve)
                    g_pCD[2 * (i % 2) + half_d] = vac
                    mk.wait("vector", vac)
            g_ad[par] = (mk.s_pe, mk.s_pe.v)
            g_he = (mk.s_pe, mk.s_pe.v)
        moe_done = (mk.s_dve, mk.s_dve.v)
        mk.flush()
        mos.close()

        if "outacc" in taps and half_idx == 0:
            mk.wait("sync", moe_done)
            mk.op("sync", lambda e: e.dma_start(tap_t["outacc"], out_acc[:]),
                  mk.s_st, 16)

        g_x2 = [0, 0]
        g_out = [0, 0]
        for i in range(16):
            par = i % 2
            mk.wait("sync", g_x2[par])
            vx = mk.op("sync", lambda e, i=i, par=par, x2=x2t, t0=tok0:
                       e.dma_start(x2[par][:],
                                   x2hbm[t0 + 128 * i:t0 + 128 * i + 128, :]),
                       mk.s_x, 16)
            mk.wait("vector", vx)
            mk.wait("vector", g_out[par])
            mk.wait("vector", moe_done)
            vad = mk.op("vector", lambda e, i=i, par=par, o=out_acc, x2=x2t:
                        e.tensor_add(x2[par][:], o[:, i, :], x2[par][:]),
                        mk.s_dve)
            g_x2[par] = vad
            mk.wait("sync", vad)
            vst = mk.op("sync", lambda e, i=i, par=par, x2=x2t, t0=tok0:
                        e.dma_start(y[t0 + 128 * i:t0 + 128 * i + 128, :],
                                    x2[par][:]), mk.s_st, 16)
            g_out[par] = vst
            g_x2[par] = vst
        mk.flush()


_NC_CACHE = {}


def _get_nc():
    if "nc" not in _NC_CACHE:
        _NC_CACHE["nc"] = build_nc()
    return _NC_CACHE["nc"]


def kernel(**inputs):
    w = prep_weights(inputs)
    x = np.ascontiguousarray(np.asarray(inputs["L0_attn_x"],
                                        dtype=np.float32))
    nc = _get_nc()
    in_maps = []
    for c in range(NCORES):
        m = {"x": np.ascontiguousarray(x[2 * c:2 * c + 2].reshape(NTOK, D))}
        m.update(w)
        in_maps.append(m)
    res = run_bass_kernel_spmd(nc, in_maps, core_ids=list(range(NCORES)))
    out = np.empty((B, T, D), dtype=np.float32)
    for c in range(NCORES):
        out[2 * c:2 * c + 2] = res.results[c]["y"].reshape(2, T, D)
    return out


# version 08
# speedup vs baseline: 1.4350x; 1.3034x over previous
import numpy as np
import ml_dtypes
from contextlib import ExitStack

import concourse.bass as bass
import concourse.mybir as mybir
from concourse.bass_utils import run_bass_kernel_spmd

F32 = mybir.dt.float32
BF16 = mybir.dt.bfloat16
AF = mybir.ActivationFunctionType
ALU = mybir.AluOpType
AX = mybir.AxisListType

B, T, D, H, Dh = 16, 2048, 1024, 16, 64
NCORES = 8
SEQS = 2
NTOK = SEQS * T
NP = H // 2
KC = D // 128
SH_M = 22
NE = 16
EPS = 1e-5

_bf16 = ml_dtypes.bfloat16



def _perm_eo(block64):
    return np.concatenate([block64[0::2], block64[1::2]], axis=0)


def prep_weights(inp):
    w_qkv = np.asarray(inp["L0_attn_w_qkv"], dtype=np.float32)
    b_qkv = np.asarray(inp["L0_attn_b_qkv"], dtype=np.float32)
    w_out = np.asarray(inp["L0_attn_w_out"], dtype=np.float32)
    b_out = np.asarray(inp["L0_attn_b_out"], dtype=np.float32)
    gate_w = np.asarray(inp["L0_ffn_gate_w"], dtype=np.float32)
    w_gu = np.asarray(inp["L0_ffn_w_gate_up"], dtype=np.float32)
    w_dn = np.asarray(inp["L0_ffn_w_down"], dtype=np.float32)
    swg = np.asarray(inp["L0_ffn_shared_w_gate"], dtype=np.float32)
    swu = np.asarray(inp["L0_ffn_shared_w_up"], dtype=np.float32)
    swd = np.asarray(inp["L0_ffn_shared_w_down"], dtype=np.float32)
    sgw = np.asarray(inp["L0_ffn_shared_gate_w"], dtype=np.float32)

    wq, wk, wv = w_qkv[0:D], w_qkv[D:2 * D], w_qkv[2 * D:3 * D]
    bq, bk, bv = b_qkv[0:D], b_qkv[D:2 * D], b_qkv[2 * D:3 * D]

    aqk = np.empty((NP, 128, KC, 2, 128), dtype=np.float32)
    bqk = np.empty((128, 2 * NP), dtype=np.float32)
    for p in range(NP):
        a, b_ = 2 * p, 2 * p + 1
        for t_i, (w, bias) in enumerate(((wq, bq), (wk, bk))):
            ea = _perm_eo(w[64 * a:64 * a + 64])
            eb = _perm_eo(w[64 * b_:64 * b_ + 64])
            rows = np.concatenate([ea[0:32], eb[0:32], ea[32:64], eb[32:64]],
                                  axis=0)
            ba = _perm_eo(bias[64 * a:64 * a + 64])
            bb = _perm_eo(bias[64 * b_:64 * b_ + 64])
            bqk[:, 2 * p + t_i] = np.concatenate(
                [ba[0:32], bb[0:32], ba[32:64], bb[32:64]], axis=0)
            for kc in range(KC):
                aqk[p, :, kc, t_i, :] = rows[:, 128 * kc:128 * kc + 128].T

    av = np.empty((NP, 128, KC, 128), dtype=np.float32)
    for p in range(NP):
        cols = wv[128 * p:128 * p + 128]
        for kc in range(KC):
            av[p, :, kc, :] = cols[:, 128 * kc:128 * kc + 128].T
    bvh = np.empty((128, KC), dtype=np.float32)
    for kc in range(KC):
        bvh[:, kc] = bv[128 * kc:128 * kc + 128]

    awout = np.empty((128, KC, D), dtype=np.float32)
    for kc in range(KC):
        awout[:, kc, :] = w_out[:, 128 * kc:128 * kc + 128].T

    g17 = np.empty((128, KC, 17), dtype=np.float32)
    for kc in range(KC):
        g17[:, kc, 0:16] = gate_w[:, 128 * kc:128 * kc + 128].T
        g17[:, kc, 16] = sgw[0, 128 * kc:128 * kc + 128]

    agu = np.empty((NE, 4, 128, KC, 2, 128), dtype=np.float32)
    for e in range(NE):
        for mp in range(4):
            for kc in range(KC):
                agu[e, mp, :, kc, 0, :] = w_gu[e, 128 * kc:128 * kc + 128,
                                               128 * mp:128 * mp + 128]
                agu[e, mp, :, kc, 1, :] = w_gu[e, 128 * kc:128 * kc + 128,
                                               512 + 128 * mp:
                                               512 + 128 * mp + 128]

    ad = np.ascontiguousarray(np.transpose(
        w_dn.reshape(NE, 4, 128, D), (0, 2, 1, 3)))

    asg = np.empty((SH_M, 128, KC, 128), dtype=np.float32)
    asu = np.empty((SH_M, 128, KC, 128), dtype=np.float32)
    for m in range(SH_M):
        for kc in range(KC):
            asg[m, :, kc, :] = swg[128 * m:128 * m + 128,
                                   128 * kc:128 * kc + 128].T
            asu[m, :, kc, :] = swu[128 * m:128 * m + 128,
                                   128 * kc:128 * kc + 128].T
    asd = np.ascontiguousarray(np.transpose(swd.reshape(D, SH_M, 128),
                                            (2, 1, 0)))

    invf = 10000.0 ** (-(np.arange(0, Dh, 2, dtype=np.float64) / Dh))
    fr = invf[:, None] * np.arange(T, dtype=np.float64)[None, :]
    c32 = np.cos(fr).astype(np.float32)
    s32 = np.sin(fr).astype(np.float32)
    css = np.ascontiguousarray(np.concatenate([c32, c32, s32, s32], axis=0))
    ssc = np.ascontiguousarray(np.concatenate([s32, s32, c32, c32], axis=0))

    return {
        "aqk": np.ascontiguousarray(aqk), "bqk": bqk,
        "av": np.ascontiguousarray(av), "bvh": bvh,
        "awout": np.ascontiguousarray(awout),
        "bout": np.ascontiguousarray(b_out[None, :]),
        "g17": np.ascontiguousarray(g17),
        "agu": np.ascontiguousarray(agu), "ad": ad,
        "asg": np.ascontiguousarray(asg), "asu": np.ascontiguousarray(asu),
        "asd": asd, "css": css, "ssc": ssc,
    }



ENGINES = ("tensor", "vector", "scalar", "gpsimd", "sync")


class Sem:
    def __init__(self, nc, name):
        self.h = nc.alloc_semaphore(name=name)
        self.v = 0

    def inc(self, n=1):
        self.v += n
        return self.v


class MK:
    def __init__(self, nc):
        self.nc = nc
        self.ops = {e: [] for e in ENGINES}
        self.s_w = Sem(nc, "s_wdma")
        self.s_x = Sem(nc, "s_xdma")
        self.s_st = Sem(nc, "s_store")
        self.s_pe = Sem(nc, "s_pe")
        self.s_act = Sem(nc, "s_act")
        self.s_dve = Sem(nc, "s_dve")

    def op(self, eng, fn, sem=None, inc=1):
        if sem is None:
            self.ops[eng].append(fn)
            return None
        v = sem.inc(inc)
        self.ops[eng].append(lambda e, fn=fn, s=sem.h, i=inc:
                             fn(e).then_inc(s, i))
        return (sem, v)

    def wait(self, eng, g):
        if not g:
            return
        sem, v = g
        if v <= 0:
            return
        self.ops[eng].append(lambda e, s=sem.h, v=v: e.wait_ge(s, v))

    def flush(self):
        ops = self.ops
        self.ops = {e: [] for e in ENGINES}

        def runner(lst):
            def f(eng):
                for o in lst:
                    o(eng)
            return f
        with self.nc.Block() as blk:
            for eng in ENGINES:
                if ops[eng]:
                    getattr(blk, eng)(runner(ops[eng]))


def _layernorm_tile(mk, src, dst_bf16, stats, sqdump):
    va = mk.op("scalar", lambda e, s=src, st=stats, d=sqdump: e.activation(
        d[:], s[:], AF.Square, accum_out=st[:, 1:2]), mk.s_act)
    vb = mk.op("vector", lambda e, s=src, st=stats: e.reduce_sum(
        st[:, 0:1], s[:], AX.X), mk.s_dve)
    mk.wait("vector", va)
    mk.op("vector", lambda e, st=stats: e.tensor_scalar_mul(
        st[:, 2:3], st[:, 0:1], 1.0 / D), None)
    vc = mk.op("vector", lambda e, st=stats: e.tensor_scalar_mul(
        st[:, 3:4], st[:, 1:2], 1.0 / D), mk.s_dve)
    mk.wait("vector", vc)
    vd = mk.op("vector", lambda e, st=stats: e.tensor_mul(
        st[:, 4:5], st[:, 2:3], st[:, 2:3]), mk.s_dve)
    mk.wait("vector", vd)
    ve = mk.op("vector", lambda e, st=stats: e.tensor_scalar(
        st[:, 5:6], st[:, 3:4], st[:, 4:5], float(EPS), ALU.subtract,
        ALU.add), mk.s_dve)
    mk.wait("scalar", ve)
    vf = mk.op("scalar", lambda e, st=stats: e.activation(
        st[:, 5:6], st[:, 5:6], AF.Sqrt), mk.s_act)
    mk.wait("vector", vf)
    vg = mk.op("vector", lambda e, st=stats: e.reciprocal(
        st[:, 6:7], st[:, 5:6]), mk.s_dve)
    mk.wait("vector", vg)
    vh = mk.op("vector", lambda e, st=stats: e.tensor_scalar(
        st[:, 7:8], st[:, 2:3], st[:, 6:7], -1.0, ALU.mult, ALU.mult),
        mk.s_dve)
    mk.wait("scalar", vh)
    vi = mk.op("scalar", lambda e, s=src, d=dst_bf16, st=stats: e.activation(
        d[:], s[:], AF.Identity, bias=st[:, 7:8], scale=st[:, 6:7]), mk.s_act)
    return vi


def _transpose_tile(mk, ln_tm, fm_dst, i, ptA, ptB, ident, pe_guard):
    last = None
    for g, pt in ((0, ptA), (1, ptB)):
        mk.wait("tensor", pe_guard[g])
        for c4 in range(4):
            c = 4 * g + c4
            mk.op("tensor", lambda e, c=c, c4=c4, pt=pt, ln=ln_tm, idn=ident:
                  e.transpose(pt[:, 128 * c4:128 * c4 + 128],
                              ln[:, 128 * c:128 * c + 128], idn[:]),
                  mk.s_pe if c4 == 3 else None)
        vpe = (mk.s_pe, mk.s_pe.v)
        mk.wait("scalar", vpe)
        v = mk.op("scalar", lambda e, g=g, pt=pt, i=i, d=fm_dst: e.copy(
            d[:, 4 * g:4 * g + 4, 128 * i:128 * i + 128], pt[:]), mk.s_act)
        pe_guard[g] = v
        last = v
    return last


def build_nc(taps=(), stop_after=None):
    nc = bass.Bass("TRN2", target_bir_lowering=False)
    taps = set(taps)

    dt_in = [
        ("x", [NTOK, D], F32), ("aqk", [NP, 128, KC, 2, 128], F32),
        ("bqk", [128, 2 * NP], F32), ("av", [NP, 128, KC, 128], F32),
        ("bvh", [128, KC], F32), ("awout", [128, KC, D], F32),
        ("bout", [1, D], F32), ("g17", [128, KC, 17], F32),
        ("agu", [NE, 4, 128, KC, 2, 128], F32), ("ad", [NE, 128, 4, D], F32),
        ("asg", [SH_M, 128, KC, 128], F32), ("asu", [SH_M, 128, KC, 128], F32),
        ("asd", [128, SH_M, D], F32), ("css", [128, T], F32),
        ("ssc", [128, T], F32),
    ]
    dr = {n: nc.dram_tensor(n, s, d, kind="ExternalInput").ap()
          for n, s, d in dt_in}
    y = nc.dram_tensor("y", [NTOK, D], F32, kind="ExternalOutput").ap()
    x2hbm = nc.dram_tensor("x2hbm", [NTOK, D], F32, kind="Internal").ap()
    ln2hbm = nc.dram_tensor("ln2hbm", [128, KC, T], BF16, kind="Internal").ap()

    tap_t = {}
    tap_specs = {
        "ln1": ("t_ln1", [128, KC, T], BF16),
        "qrot": ("t_qrot", [128, T], BF16), "krot": ("t_krot", [128, T], BF16),
        "attn": ("t_attn", [128, KC, T], BF16),
        "ln2": ("t_ln2", [128, KC, T], BF16),
        "ctm": ("t_ctm", [128, 32, 16], F32), "gtm": ("t_gtm", [128, 32], F32),
        "outacc": ("t_outacc", [128, 16, D], F32),
        "vtap": ("t_vtap", [128, 16, 65], BF16),
        "x2": ("t_x2", [NTOK, D], F32),
        "shfm": ("t_shfm", [128, 8, T], BF16),
        "outacc1": ("t_outacc1", [128, 16, D], F32),
        "ln2ffn": ("t_ln2ffn", [128, KC, T], BF16),
        "gtm2": ("t_gtm2", [128, 32], F32),
        "dbg1": ("t_dbg1", [128, 1], F32),
        "dbg2": ("t_dbg2", [128, 512], BF16),
        "dbg3": ("t_dbg3", [1, 512], F32),
        "dbg4": ("t_dbg4", [32, 8], F32),
        "dbg5": ("t_dbg5", [2, T], F32),
    }
    for k in taps:
        name, shape, dt = tap_specs[k]
        tap_t[k] = nc.dram_tensor(name, shape, dt, kind="ExternalOutput").ap()

    ident_d = nc.inline_tensor(np.eye(128, dtype=_bf16), name="identc").ap()
    onesrow_d = nc.inline_tensor(np.ones((1, 128), dtype=_bf16),
                                 name="onesrowc").ap()
    o2 = np.zeros((128, 2), dtype=_bf16)
    o2[0:64, 0] = 1
    o2[64:128, 1] = 1
    ones2_d = nc.inline_tensor(o2, name="ones2c").ap()
    identf_d = nc.inline_tensor(np.eye(128, dtype=np.float32),
                                name="identfc").ap()
    _r = np.arange(128)
    m4q_d = nc.inline_tensor(
        np.where(_r[None, :] > _r[:, None], -1e9, 0.0).astype(np.float32),
        name="m4qc").ap()
    m4k_d = nc.inline_tensor(
        np.where(_r[:, None] > _r[None, :], -1e9, 0.0).astype(np.float32),
        name="m4kc").ap()

    mk = MK(nc)
    sb = nc.sbuf_tensor
    ps = nc.psum_tensor
    with ExitStack() as es:
        ident = es.enter_context(sb("ident", [128, 128], BF16))
        onesrow = es.enter_context(sb("onesrow", [1, 128], BF16))
        ones2 = es.enter_context(sb("ones2", [128, 2], BF16))
        bqk_sb = es.enter_context(sb("bqk_sb", [128, 2 * NP], F32))
        bfull = es.enter_context(sb("bfull", [128, D], F32))
        g17_sb = es.enter_context(sb("g17_sb", [128, KC, 17], BF16))
        c_tm = es.enter_context(sb("c_tm", [128, 32, 16], F32))
        g_tm = es.enter_context(sb("g_tm", [128, 32], F32))
        ln2buf = es.enter_context(sb("ln2buf", [128, KC, T], BF16))

        with ExitStack() as ss:
            awout_sb = ss.enter_context(sb("awout0", [128, KC, D], BF16))
            bvh_sb = ss.enter_context(sb("bvh_sb", [128, KC], F32))
            bvb = ss.enter_context(sb("bvb", [128, KC], BF16))
            bout_sb = ss.enter_context(sb("bout_sb", [1, D], F32))
            brow = ss.enter_context(sb("brow", [1, D], F32))
            browb = ss.enter_context(sb("browb", [1, D], BF16))
            pb = ss.enter_context(ps("pb", [128, 512], F32))
            pb2 = ss.enter_context(ps("pb2", [128, 512], F32))

            mk.op("sync", lambda e: e.dma_start(ident[:], ident_d), mk.s_x, 16)
            vor = mk.op("sync", lambda e: e.dma_start(onesrow[:], onesrow_d),
                        mk.s_x, 16)
            mk.op("sync", lambda e: e.dma_start(ones2[:], ones2_d),
                  mk.s_x, 16)
            mk.op("sync", lambda e: e.dma_start(bqk_sb[:], dr["bqk"]),
                  mk.s_x, 16)
            mk.op("gpsimd", lambda e: e.dma_start(g17_sb[:], dr["g17"]),
                  mk.s_w, 16)
            va = mk.op("gpsimd", lambda e: e.dma_start(awout_sb[:],
                                                       dr["awout"]),
                       mk.s_w, 16)
            vb = mk.op("sync", lambda e: e.dma_start(bvh_sb[:], dr["bvh"]),
                       mk.s_x, 16)
            vo = mk.op("sync", lambda e: e.dma_start(bout_sb[:], dr["bout"]),
                       mk.s_x, 16)
            mk.wait("scalar", vb)
            vc = mk.op("scalar", lambda e: e.copy(bvb[:], bvh_sb[:]),
                       mk.s_act)
            mk.wait("tensor", vc)
            mk.wait("tensor", va)
            for half in range(2):
                dst = pb if half == 0 else pb2
                for kc in range(KC):
                    mk.op("tensor", lambda e, kc=kc, h=half, d=dst: e.matmul(
                        d[0:1, :], bvb[:, kc:kc + 1],
                        awout_sb[:, kc, 512 * h:512 * h + 512],
                        start=(kc == 0), stop=(kc == KC - 1)),
                        mk.s_pe if kc == KC - 1 else None)
            vpe = (mk.s_pe, mk.s_pe.v)
            mk.wait("vector", vpe)
            mk.wait("vector", vo)
            mk.op("vector", lambda e: e.tensor_add(
                brow[0:1, 0:512], pb[0:1, :], bout_sb[0:1, 0:512]), None)
            vd = mk.op("vector", lambda e: e.tensor_add(
                brow[0:1, 512:1024], pb2[0:1, :], bout_sb[0:1, 512:1024]),
                mk.s_dve)
            mk.wait("scalar", vd)
            vc2 = mk.op("scalar", lambda e: e.copy(browb[:], brow[:]),
                        mk.s_act)
            mk.wait("tensor", vc2)
            mk.wait("tensor", vor)
            mk.wait("tensor", vd)
            for half in range(2):
                dst = pb if half == 0 else pb2
                mk.op("tensor", lambda e, h=half, d=dst: e.matmul(
                    d[:], onesrow[0:1, :], browb[0:1, 512 * h:512 * h + 512],
                    start=True, stop=True), mk.s_pe)
            vpe2 = (mk.s_pe, mk.s_pe.v)
            mk.wait("scalar", vpe2)
            mk.op("scalar", lambda e: e.copy(bfull[:, 0:512], pb[:]),
                  mk.s_act)
            mk.op("scalar", lambda e: e.copy(bfull[:, 512:1024], pb2[:]),
                  mk.s_act)
            mk.flush()

        with ExitStack() as att:
            css = att.enter_context(sb("css_sb", [128, T], F32))
            ssc = att.enter_context(sb("ssc_sb", [128, T], F32))
            identf = att.enter_context(sb("identf", [128, 128], F32))
            m4q_sb = att.enter_context(sb("m4q_sb", [128, 128], F32))
            m4k_sb = att.enter_context(sb("m4k_sb", [128, 128], F32))
            ln1_fm = att.enter_context(sb("ln1_fm", [128, KC, T], BF16))
            attn_fm = att.enter_context(sb("attn_fm", [128, KC, T], BF16))

            mk.op("sync", lambda e: e.dma_start(css[:], dr["css"]),
                  mk.s_x, 16)
            mk.op("sync", lambda e: e.dma_start(ssc[:], dr["ssc"]),
                  mk.s_x, 16)
            mk.op("sync", lambda e: e.dma_start(identf[:], identf_d),
                  mk.s_x, 16)
            mk.op("sync", lambda e: e.dma_start(m4q_sb[:], m4q_d),
                  mk.s_x, 16)
            mk.op("sync", lambda e: e.dma_start(m4k_sb[:], m4k_d),
                  mk.s_x, 16)
            mk.flush()

            for seq in range(SEQS):
                _attn_seq(mk, sb, ps, dr, seq, css, ssc, m4q_sb, m4k_sb,
                          identf, ln1_fm, attn_fm, ln2buf, bqk_sb, bfull,
                          ident, onesrow, x2hbm, ln2hbm, tap_t, taps,
                          stop_after)
                if stop_after in ("ln1", "attn") and seq == 0:
                    break

        if stop_after in ("ln1", "attn", "a3"):
            if "x2" in taps and stop_after == "a3":
                mk.op("sync", lambda e: e.dma_start(tap_t["x2"], x2hbm),
                      mk.s_st, 16)
            mk.wait("sync", (mk.s_st, mk.s_st.v))
            mk.flush()
            return nc

        with ExitStack() as fes:
            _ffn_phase(mk, fes, sb, ps, dr, ln2buf, ln2hbm, g17_sb, c_tm,
                       g_tm, x2hbm, y, tap_t, taps, stop_after)

        if "x2" in taps:
            mk.op("sync", lambda e: e.dma_start(tap_t["x2"], x2hbm),
                  mk.s_st, 16)
        mk.wait("sync", (mk.s_st, mk.s_st.v))
        mk.flush()
    return nc


def _attn_seq(mk, sb, ps, dr, seq, css, ssc, m4q_sb, m4k_sb, identf,
              ln1_fm, attn_fm, ln2buf, bqk_sb, bfull, ident, onesrow,
              x2hbm, ln2hbm, tap_t, taps, stop_after):
    tok0 = seq * T

    with ExitStack() as ss:
        xt = [ss.enter_context(sb(f"xt{seq}_{i}", [128, D], F32))
              for i in range(2)]
        lnt = [ss.enter_context(sb(f"lnt{seq}_{i}", [128, D], BF16))
               for i in range(2)]
        stats = [ss.enter_context(sb(f"lnstat{seq}_{i}", [128, 8], F32))
                 for i in range(2)]
        sqdump = ss.enter_context(sb(f"sqd{seq}", [128, D], BF16))
        ptA = ss.enter_context(ps(f"ptA{seq}", [128, 512], BF16))
        ptB = ss.enter_context(ps(f"ptB{seq}", [128, 512], BF16))
        pe_guard = {0: 0, 1: 0}
        buf_free = [0, 0]
        ln_free = [0, 0]
        for i in range(T // 128):
            par = i % 2
            mk.wait("sync", buf_free[par])
            vx = mk.op("sync", lambda e, i=i, par=par, xt=xt, t0=tok0:
                       e.dma_start(xt[par][:],
                                   dr["x"][t0 + 128 * i:t0 + 128 * i + 128,
                                           :]), mk.s_x, 16)
            mk.wait("scalar", vx)
            mk.wait("vector", vx)
            mk.wait("scalar", ln_free[par])
            vi = _layernorm_tile(mk, xt[par], lnt[par], stats[par], sqdump)
            buf_free[par] = vi
            mk.wait("tensor", vi)
            vtr = _transpose_tile(mk, lnt[par], ln1_fm, i, ptA, ptB, ident,
                                  pe_guard)
            ln_free[par] = vtr
        mk.flush()

    if "ln1" in taps and seq == 0:
        mk.op("sync", lambda e: e.dma_start(tap_t["ln1"], ln1_fm[:]),
              mk.s_st, 16)
        mk.wait("sync", (mk.s_st, mk.s_st.v))
        mk.flush()
    if stop_after == "ln1":
        return
    with ExitStack() as ss:
        wqk = [ss.enter_context(sb(f"wqk{seq}_{i}", [128, KC, 2, 128], BF16))
               for i in range(2)]
        avp = [ss.enter_context(sb(f"avp{seq}_{i}", [128, KC, 128], BF16))
               for i in range(2)]
        qst = ss.enter_context(sb(f"qst{seq}", [128, 512], F32))
        kst = ss.enter_context(sb(f"kst{seq}", [128, 512], F32))
        mt1 = ss.enter_context(sb(f"mt1{seq}", [128, 512], F32))
        mt2 = ss.enter_context(sb(f"mt2{seq}", [128, 512], F32))
        mo = ss.enter_context(sb(f"mo{seq}", [64, 512], F32))
        me = ss.enter_context(sb(f"me{seq}", [64, 512], F32))
        qrot = ss.enter_context(sb(f"qrot{seq}", [128, T], BF16))
        krot = ss.enter_context(sb(f"krot{seq}", [128, T], BF16))
        qag = [[ss.enter_context(sb(f"qag{seq}_{pr}_{h}", [65, T], BF16))
                for h in range(2)] for pr in range(2)]
        kag = [[ss.enter_context(sb(f"kag{seq}_{pr}_{h}", [65, T], BF16))
                for h in range(2)] for pr in range(2)]
        macc = ss.enter_context(sb(f"macc{seq}", [128, 4], F32))
        cmx = ss.enter_context(sb(f"cmx{seq}", [128, 1], F32))
        vh = [[ss.enter_context(sb(f"vh{seq}_{pr}_{h}", [128, 16, 65], BF16))
               for h in range(2)] for pr in range(2)]
        prb = [ss.enter_context(sb(f"prb{seq}_{hj}", [128, 512], BF16))
               for hj in range(4)]
        invzr = ss.enter_context(sb(f"invzr{seq}", [1, 512], F32))
        invzb = ss.enter_context(sb(f"invzb{seq}", [1, 512], BF16))
        invzbc = ss.enter_context(sb(f"invzbc{seq}", [64, 512], F32))
        pq = ss.enter_context(ps(f"pq{seq}", [128, 512], F32))
        pk = ss.enter_context(ps(f"pk{seq}", [128, 512], F32))
        psA = [ss.enter_context(ps(f"psA{seq}_{j}", [128, 512], F32))
               for j in range(2)]
        psB = [ss.enter_context(ps(f"psB{seq}_{j}", [128, 512], F32))
               for j in range(2)]
        ppv = [ss.enter_context(ps(f"ppv{seq}_{h}", [128, 512], F32))
               for h in range(2)]

        for pr in range(2):
            mk.op("vector", lambda e, v=vh, pr=pr: e.memset(
                v[pr][0][:, :, 64:65], 1.0), None)
            mk.op("vector", lambda e, v=vh, pr=pr: e.memset(
                v[pr][1][:, :, 64:65], 1.0), None)
            mk.op("vector", lambda e, k=kag, pr=pr: e.memset(
                k[pr][0][64:65, :], 1.0), None)
            mk.op("vector", lambda e, k=kag, pr=pr: e.memset(
                k[pr][1][64:65, :], 1.0), mk.s_dve)
        memset_done = (mk.s_dve, mk.s_dve.v)

        G = {
            "wqk": [0, 0], "avp": [0, 0],
            "pq": 0, "pk": 0,
            "psA": [0, 0], "psB": [0, 0],
            "ppv": [0, 0],
            "prb": [0, 0, 0, 0],
            "rot_pe": 0,
            "aug_pe": [0, 0],
            "vh_pe": [0, 0],
            "stage": 0,
            "invzbc": 0,
            "piece_done": {},
            "v_piece": 0,
            "rope_done": 0,
        }

        def stage_a_gen(p):
          par = p % 2
          for u in range(4):
            if u == 0:
                mk.wait("gpsimd", G["wqk"][par])
                vw = mk.op("gpsimd", lambda e, p=p, par=par, w=wqk:
                           e.dma_start(w[par][:], dr["aqk"][p]), mk.s_w, 16)
                G["wqk_dma"] = vw
                mk.wait("gpsimd", G["avp"][par])
                vav = mk.op("gpsimd", lambda e, p=p, par=par, a=avp:
                            e.dma_start(a[par][:], dr["av"][p]), mk.s_w, 16)
                G["avp_dma"] = vav
            n = u
            if u == 0:
                mk.wait("tensor", G["wqk_dma"])
            for t_i in range(2):
                dst = pq if t_i == 0 else pk
                mk.wait("tensor", G["pq"] if t_i == 0 else G["pk"])
                for kc in range(KC):
                    mk.op("tensor", lambda e, n=n, kc=kc, t_i=t_i, par=par,
                          dst=dst, w=wqk, lf=ln1_fm: e.matmul(
                              dst[:], w[par][:, kc, t_i, :],
                              lf[:, kc, 512 * n:512 * n + 512],
                              start=(kc == 0), stop=(kc == KC - 1)),
                          mk.s_pe if kc == KC - 1 else None)
                vmm = (mk.s_pe, mk.s_pe.v)
                mk.wait("scalar", vmm)
                mk.wait("scalar", G["stage"])
                dstf = qst if t_i == 0 else kst
                bia = 2 * p + t_i
                vev = mk.op("scalar", lambda e, dstf=dstf, bia=bia, dst=dst,
                            bq=bqk_sb: e.activation(
                                dstf[:], dst[:], AF.Identity,
                                bias=bq[:, bia:bia + 1]), mk.s_act)
                if t_i == 0:
                    G["pq"] = vev
                else:
                    G["pk"] = vev
                yield
            mk.wait("vector", G["pk"])
            if n == 0:
                mk.wait("vector", G["rot_pe"])
                mk.wait("vector", G["aug_pe"][par])
                mk.wait("vector", memset_done)
            for srcf, dstrot, ag in ((qst, qrot, qag[par]),
                                     (kst, krot, kag[par])):
                sl = slice(512 * n, 512 * n + 512)
                mk.op("vector", lambda e, s=srcf, sl=sl, m=mt1, c=css:
                      e.tensor_mul(m[:], s[:], c[:, sl]), None)
                v1 = mk.op("vector", lambda e, s=srcf, sl=sl, m=mt2, c=ssc:
                           e.tensor_mul(m[:], s[:], c[:, sl]), mk.s_dve)
                mk.wait("vector", v1)
                mk.op("vector", lambda e, m=mt1, o=mo: e.tensor_copy(
                    o[:], m[64:128, :]), None)
                v2 = mk.op("vector", lambda e, m=mt2, o=me: e.tensor_copy(
                    o[:], m[64:128, :]), mk.s_dve)
                mk.wait("vector", v2)
                mk.op("vector", lambda e, d=dstrot, sl=sl, m=mt1, o=mo:
                      e.tensor_sub(d[0:32, sl], m[0:32, :], o[0:32, :]),
                      None)
                mk.op("vector", lambda e, d=dstrot, sl=sl, m=mt2, o=me:
                      e.tensor_add(d[32:64, sl], m[0:32, :], o[0:32, :]),
                      None)
                mk.op("vector", lambda e, d=dstrot, sl=sl, m=mt1, o=mo:
                      e.tensor_sub(d[64:96, sl], m[32:64, :], o[32:64, :]),
                      None)
                v3 = mk.op("vector", lambda e, d=dstrot, sl=sl, m=mt2, o=me:
                           e.tensor_add(d[96:128, sl], m[32:64, :],
                                        o[32:64, :]), mk.s_dve)
                mk.wait("vector", v3)
                mk.op("vector", lambda e, d=dstrot, sl=sl, a=ag:
                      e.tensor_copy(a[0][0:64, sl], d[0:64, sl]), None)
                v4 = mk.op("vector", lambda e, d=dstrot, sl=sl, a=ag:
                           e.tensor_copy(a[1][0:64, sl], d[64:128, sl]),
                           mk.s_dve)
                mk.wait("vector", v4)
                yield
            G["stage"] = (mk.s_dve, mk.s_dve.v)
            if n == 3:
                G["rope_done"] = (mk.s_dve, mk.s_dve.v)

            bank = pq if u % 2 == 0 else pk
            bank_g = "pq" if u % 2 == 0 else "pk"
            if u == 0:
                mk.wait("tensor", G["avp_dma"])
                mk.wait("vector", G["vh_pe"][par])
            mk.wait("tensor", G[bank_g])
            for ii in range(4):
                i = 4 * u + ii
                for kc in range(KC):
                    mk.op("tensor", lambda e, i=i, ii=ii, kc=kc, par=par,
                          dstp=bank, a=avp, lf=ln1_fm: e.matmul(
                              dstp[:, 128 * ii:128 * ii + 128],
                              lf[:, kc, 128 * i:128 * i + 128],
                              a[par][:, kc, :],
                              start=(ii == 0 and kc == 0),
                              stop=(ii == 3 and kc == KC - 1)),
                          mk.s_pe if (ii == 3 and kc == KC - 1) else None)
                if ii == 1:
                    yield
            vv = (mk.s_pe, mk.s_pe.v)
            mk.wait("vector", vv)
            for h in range(2):
                vve = mk.op("vector", lambda e, u=u, h=h, par=par,
                            dstp=bank, v=vh: e.tensor_copy(
                                v[par][h][:, 4 * u:4 * u + 4, 0:64],
                                dstp[:].rearrange(
                                    "p (t c) -> p t c", t=4)[:, :,
                                                             64 * h:
                                                             64 * h + 64]),
                            mk.s_dve)
            G[bank_g] = vve
            G["v_piece"] = vve
            yield

            mk.wait("tensor", G["stage"])
            for ii in range(4):
                i = 4 * u + ii
                nk = i // 4 + 1
                u_d = i % 4
                for h in range(2):
                    r0 = 64 * h
                    for n in range(nk):
                        G["p1c"] = G.get("p1c", 0) + 1
                        bank = pq if G["p1c"] % 2 == 0 else pk
                        bank_g = "pq" if G["p1c"] % 2 == 0 else "pk"
                        mk.wait("tensor", G[bank_g])
                        mk.op("tensor", lambda e, i=i, n=n, r0=r0,
                              dstp=bank, qr=qrot, kr=krot: e.matmul(
                                  dstp[:], qr[r0:r0 + 64,
                                              128 * i:128 * i + 128],
                                  kr[r0:r0 + 64, 512 * n:512 * n + 512],
                                  start=True, stop=True), mk.s_pe)
                        vqk = (mk.s_pe, mk.s_pe.v)
                        mk.wait("vector", vqk)
                        if n == i // 4:
                            vmk = mk.op("vector", lambda e, u_d=u_d,
                                        dstp=bank, m=m4q_sb: e.tensor_add(
                                            dstp[:, 128 * u_d:
                                                 128 * u_d + 128],
                                            dstp[:, 128 * u_d:
                                                 128 * u_d + 128],
                                            m[:]), mk.s_dve)
                            mk.wait("vector", vmk)
                            red_sl = slice(0, 128 * (u_d + 1))
                        else:
                            red_sl = slice(0, 512)
                        if n == 0:
                            vred = mk.op("vector", lambda e, h=h, sl=red_sl,
                                         dstp=bank, mc=macc: e.reduce_max(
                                             mc[:, h:h + 1], dstp[:, sl],
                                             AX.X), mk.s_dve)
                        else:
                            vred = mk.op("vector", lambda e, sl=red_sl,
                                         dstp=bank, c=cmx: e.reduce_max(
                                             c[:], dstp[:, sl], AX.X),
                                         mk.s_dve)
                            mk.wait("vector", vred)
                            vred = mk.op("vector", lambda e, h=h, mc=macc,
                                         c=cmx: e.tensor_max(
                                             mc[:, h:h + 1], mc[:, h:h + 1],
                                             c[:]), mk.s_dve)
                        mk.wait("vector", vred)
                        G[bank_g] = vred
                        yield
                mk.wait("vector", G.get("macc", 0))
                vng = mk.op("vector", lambda e, mc=macc: e.tensor_scalar_mul(
                    mc[:, 2:4], mc[:, 0:2], -1.0), mk.s_dve)
                mk.wait("tensor", vng)
                mk.wait("tensor", G[bank_g])
                for h in range(2):
                    mk.op("tensor", lambda e, h=h, mc=macc, d=bank,
                          idf=identf: e.transpose(
                              d[0:1, 128 * h:128 * h + 128],
                              mc[:, 2 + h:3 + h], idf[:]), mk.s_pe)
                vtr = (mk.s_pe, mk.s_pe.v)
                G["macc"] = vtr
                mk.wait("scalar", vtr)
                mk.op("scalar", lambda e, i=i, par=par, q=qag, d=bank:
                      e.copy(q[par][0][64:65, 128 * i:128 * i + 128],
                             d[0:1, 0:128]), None)
                vev = mk.op("scalar", lambda e, i=i, par=par, q=qag, d=bank:
                            e.copy(q[par][1][64:65, 128 * i:128 * i + 128],
                                   d[0:1, 128:256]), mk.s_act)
                G[bank_g] = vev
                yield
            G["piece_done"][(p, u)] = [vev, G["v_piece"], G["stage"]]
            if u == 3:
                G["rot_pe"] = (mk.s_pe, mk.s_pe.v)
            yield ("pair_boundary" if u == 3 else "piece")

        def stage_b(p, s, drip):
            par = p % 2
            nj = 4 * s + 4
            mk.wait("tensor", G["ppv"][0])
            mk.wait("tensor", G["ppv"][1])
            for gref in G["piece_done"].pop((p, s)):
                mk.wait("tensor", gref)
            prb_ready = {}

            def emit_qk(j):
                jp = j % 2
                diag_r = j - 4 * s
                for h in range(2):
                    pst = psA if h == 0 else psB
                    mk.wait("tensor", G["psA"][jp] if h == 0 else
                            G["psB"][jp])
                    mk.op("tensor", lambda e, j=j, s=s, h=h, par=par,
                          dstp=pst[jp], ka=kag, qa=qag: e.matmul(
                              dstp[:], ka[par][h][:, 128 * j:128 * j + 128],
                              qa[par][h][:, 512 * s:512 * s + 512],
                              start=True, stop=True), mk.s_pe)
                    vqk = (mk.s_pe, mk.s_pe.v)
                    pbuf = prb[2 * h + jp]
                    mk.wait("scalar", vqk)
                    mk.wait("scalar", G["prb"][2 * h + jp])
                    if diag_r >= 0:
                        mk.wait("vector", vqk)
                        mk.wait("vector", G["prb"][2 * h + jp])
                        vdg = mk.op("vector", lambda e, r=diag_r,
                                    dstp=pst[jp], m=m4k_sb: e.tensor_add(
                                        dstp[:, 128 * r:128 * r + 128],
                                        dstp[:, 128 * r:128 * r + 128],
                                        m[:]), mk.s_dve)
                        if diag_r > 0:
                            vz = mk.op("vector", lambda e, r=diag_r,
                                       pbuf=pbuf: e.memset(
                                           pbuf[:, 0:128 * r], 0.0),
                                       mk.s_dve)
                        mk.wait("scalar", vdg)
                        if diag_r > 0:
                            mk.wait("scalar", vz)
                        sl_e = slice(128 * diag_r, 512)
                    else:
                        sl_e = slice(0, 512)
                    vex = mk.op("scalar", lambda e, pbuf=pbuf, sl=sl_e,
                                dstp=pst[jp]: e.activation(
                                    pbuf[:, sl], dstp[:, sl], AF.Exp,
                                    scale=0.125), mk.s_act)
                    if h == 0:
                        G["psA"][jp] = vex
                    else:
                        G["psB"][jp] = vex
                    prb_ready[(j, h)] = vex

            def emit_pv(j):
                jp = j % 2
                for h in range(2):
                    pbuf = prb[2 * h + jp]
                    mk.wait("tensor", prb_ready.pop((j, h)))
                    mk.op("tensor", lambda e, h=h, j=j, nj=nj, par=par,
                          pbuf=pbuf, v=vh, pv=ppv: e.matmul(
                              pv[h][0:65, :], v[par][h][:, j, :], pbuf[:],
                              start=(j == 0), stop=(j == nj - 1)),
                          mk.s_pe)
                    G["prb"][2 * h + jp] = (mk.s_pe, mk.s_pe.v)

            for j in range(nj):
                emit_qk(j)
                if j > 0:
                    emit_pv(j - 1)
                drip()
            emit_pv(nj - 1)
            vpv_done = (mk.s_pe, mk.s_pe.v)

            for h in range(2):
                mk.wait("scalar", vpv_done)
                mk.wait("scalar", G["invzr"] if "invzr" in G else 0)
                vln = mk.op("scalar", lambda e, h=h, pv=ppv, iz=invzr:
                            e.activation(iz[0:1, :], pv[h][64:65, :],
                                         AF.Ln), mk.s_act)
                mk.wait("scalar", vln)
                vcb = mk.op("scalar", lambda e, iz=invzr, ib=invzb:
                            e.activation(ib[:], iz[0:1, :], AF.Exp,
                                         scale=-1.0), mk.s_act)
                G["invzr"] = vcb
                mk.wait("tensor", vcb)
                mk.op("tensor", lambda e, h=h, o=onesrow, ib=invzb,
                      pv=ppv: e.matmul(
                          pv[h][64:128, :], o[0:1, 0:64],
                          ib[0:1, :], start=True, stop=True,
                          tile_position=(0, 64)), mk.s_pe)
                vbc = (mk.s_pe, mk.s_pe.v)
                mk.wait("scalar", vbc)
                mk.wait("scalar", G["invzbc"])
                vzc = mk.op("scalar", lambda e, h=h, pv=ppv, z=invzbc:
                            e.copy(z[:], pv[h][64:128, :]), mk.s_act)
                mk.wait("vector", vzc)
                vnr = mk.op("vector", lambda e, h=h, p=p, s=s, pv=ppv,
                            z=invzbc, af=attn_fm: e.tensor_mul(
                                af[64 * h:64 * h + 64, p,
                                   512 * s:512 * s + 512],
                                pv[h][0:64, :], z[:]), mk.s_dve)
                G["ppv"][h] = vnr
                G["invzbc"] = vnr
                drip()
            if s == 3:
                G["aug_pe"][par] = (mk.s_pe, mk.s_pe.v)
                G["vh_pe"][par] = (mk.s_pe, mk.s_pe.v)
                G["wqk"][par] = (mk.s_pe, mk.s_pe.v)
                G["avp"][par] = (mk.s_pe, mk.s_pe.v)

        import itertools as _it
        gen = _it.chain.from_iterable(stage_a_gen(pp) for pp in range(NP))
        state = {"gen_pair": 0, "done": False}

        def pull():
            if state["done"]:
                return False
            q = next(gen, "end")
            if q == "end":
                state["done"] = True
                return False
            if q == "pair_boundary":
                state["gen_pair"] += 1
            return True

        def make_drip(p):
            budget = [0]

            def drip():
                budget[0] += 32
                while budget[0] >= 20 and state["gen_pair"] <= p + 1:
                    budget[0] -= 20
                    if not pull():
                        break
            return drip
        for p in range(NP):
            drip = make_drip(p)
            for s in range(4):
                while (p, s) in G["piece_done"] or state["done"]:
                    break
                while (p, s) not in G["piece_done"]:
                    if not pull():
                        break
                stage_b(p, s, drip)

        if seq == 0 and "qrot" in taps:
            mk.wait("sync", (mk.s_dve, mk.s_dve.v))
            mk.op("sync", lambda e: e.dma_start(tap_t["qrot"], qrot[:]),
                  mk.s_st, 16)
            mk.op("sync", lambda e: e.dma_start(tap_t["krot"], krot[:]),
                  mk.s_st, 16)
            mk.wait("sync", (mk.s_st, mk.s_st.v))
        mk.flush()

    if "attn" in taps and seq == 0:
        mk.op("sync", lambda e: e.dma_start(tap_t["attn"], attn_fm[:]),
              mk.s_st, 16)
        mk.wait("sync", (mk.s_st, mk.s_st.v))
        mk.flush()
    if stop_after == "attn":
        return

    with ExitStack() as ss:
        awout_sb = ss.enter_context(sb(f"awo{seq}", [128, KC, D], BF16))
        xt = [ss.enter_context(sb(f"x3t{seq}_{i}", [128, D], F32))
              for i in range(2)]
        x2t = [ss.enter_context(sb(f"x2t{seq}_{i}", [128, D], F32))
               for i in range(2)]
        ln2t = [ss.enter_context(sb(f"ln2t{seq}_{i}", [128, D], BF16))
                for i in range(2)]
        stats = [ss.enter_context(sb(f"lnst2{seq}_{i}", [128, 8], F32))
                 for i in range(2)]
        sqdump = ss.enter_context(sb(f"sqd2{seq}", [128, D], BF16))
        po = [ss.enter_context(ps(f"po{seq}_{i}", [128, 512], F32))
              for i in range(4)]
        ptA = ss.enter_context(ps(f"pt2A{seq}", [128, 512], BF16))
        ptB = ss.enter_context(ps(f"pt2B{seq}", [128, 512], BF16))

        vaw = mk.op("gpsimd", lambda e, a=awout_sb: e.dma_start(
            a[:], dr["awout"]), mk.s_w, 16)
        mk.wait("tensor", vaw)
        pe_guard = {0: 0, 1: 0}
        g_po = [0, 0, 0, 0]
        buf_free = [0, 0]
        x2_free = [0, 0]
        ln_free = [0, 0]
        for i in range(T // 128):
            par = i % 2
            mk.wait("sync", buf_free[par])
            vx = mk.op("sync", lambda e, i=i, par=par, xt=xt, t0=tok0:
                       e.dma_start(xt[par][:],
                                   dr["x"][t0 + 128 * i:t0 + 128 * i + 128,
                                           :]), mk.s_x, 16)
            for half in range(2):
                dstp = po[2 * par + half]
                mk.wait("tensor", g_po[2 * par + half])
                for kc in range(KC):
                    mk.op("tensor", lambda e, i=i, kc=kc, half=half,
                          dstp=dstp, af=attn_fm, aw=awout_sb: e.matmul(
                              dstp[:], af[:, kc, 128 * i:128 * i + 128],
                              aw[:, kc, 512 * half:512 * half + 512],
                              start=(kc == 0), stop=(kc == KC - 1)),
                          mk.s_pe if kc == KC - 1 else None)
            vmm = (mk.s_pe, mk.s_pe.v)
            mk.wait("vector", vmm)
            mk.wait("vector", vx)
            mk.wait("vector", x2_free[par])
            for half in range(2):
                sl = slice(512 * half, 512 * half + 512)
                dstp = po[2 * par + half]
                mk.op("vector", lambda e, sl=sl, par=par, dstp=dstp, x2=x2t,
                      bf=bfull: e.tensor_add(x2[par][:, sl], dstp[:],
                                             bf[:, sl]), None)
            vadd = mk.op("vector", lambda e, par=par, x2=x2t, xt=xt:
                         e.tensor_add(x2[par][:], x2[par][:], xt[par][:]),
                         mk.s_dve)
            g_po[2 * par] = vadd
            g_po[2 * par + 1] = vadd
            buf_free[par] = vadd
            mk.wait("sync", vadd)
            vsp = mk.op("sync", lambda e, i=i, par=par, x2=x2t, t0=tok0:
                        e.dma_start(x2hbm[t0 + 128 * i:t0 + 128 * i + 128, :],
                                    x2[par][:]), mk.s_st, 16)
            x2_free[par] = vsp
            mk.wait("scalar", vadd)
            mk.wait("vector", vadd)
            mk.wait("scalar", ln_free[par])
            vi = _layernorm_tile(mk, x2t[par], ln2t[par], stats[par], sqdump)
            mk.wait("tensor", vi)
            vtr = _transpose_tile(mk, ln2t[par], ln2buf, i, ptA, ptB, ident,
                                  pe_guard)
            ln_free[par] = vtr
        if seq == 0:
            mk.wait("sync", ln_free[0])
            mk.wait("sync", ln_free[1])
            mk.op("sync", lambda e: e.dma_start(ln2hbm, ln2buf[:]),
                  mk.s_st, 16)
        mk.flush()

    if "ln2" in taps and seq == 0:
        mk.op("sync", lambda e: e.dma_start(tap_t["ln2"], ln2buf[:]),
              mk.s_st, 16)
        mk.wait("sync", (mk.s_st, mk.s_st.v))
        mk.flush()


def _ffn_phase(mk, es, sb, ps, dr, ln2buf, ln2hbm, g17_sb, c_tm, g_tm,
               x2hbm, y, tap_t, taps, stop_after):
    out_acc = es.enter_context(sb("out_acc", [128, 16, D], F32))
    lgt = es.enter_context(sb("lgt", [128, 17], F32))
    r_sc = es.enter_context(sb("r_sc", [128, 8], F32))
    e1t = es.enter_context(sb("e1t", [128, 16], F32))
    is1t = es.enter_context(sb("is1t", [128, 16], F32))
    l2t = es.enter_context(sb("l2t", [128, 16], F32))
    indt = es.enter_context(sb("indt", [128, 16], F32))
    sg2 = [es.enter_context(sb(f"sg_{i}", [128, 512], F32))
           for i in range(2)]
    sh_fm = es.enter_context(sb("sh_fm", [128, 8, T], BF16))
    h_e2 = [es.enter_context(sb(f"h_e{i}", [128, 4, T], BF16))
            for i in range(2)]
    x2t = [es.enter_context(sb(f"x2b{i}", [128, D], F32)) for i in range(2)]

    pA = es.enter_context(ps("pfA", [128, 512], F32))
    pB = es.enter_context(ps("pfB", [128, 512], F32))
    pG = es.enter_context(ps("pfG", [128, 512], F32))
    pH = es.enter_context(ps("pfH", [128, 512], F32))
    pCD = [es.enter_context(ps(f"pf{c}", [128, 512], F32))
           for c in ("C", "D", "E", "F")]

    SH_BLOCKS = ((0, 8), (8, 16), (16, 22))

    for half_idx, half in enumerate((1, 0)):
        tok0 = half * T
        til0 = half * 16
        if half_idx == 1:
            vld = mk.op("sync", lambda e: e.dma_start(ln2buf[:], ln2hbm),
                        mk.s_x, 16)
            mk.wait("tensor", vld)

        if "ln2ffn" in taps and half_idx == 0:
            mk.op("sync", lambda e: e.dma_start(tap_t["ln2ffn"], ln2buf[:]),
                  mk.s_st, 16)
            mk.wait("sync", (mk.s_st, mk.s_st.v))
        g_pA = 0
        g_rt_dve = 0
        g_rt_act = 0
        for i in range(16):
            mk.wait("tensor", g_pA)
            for kc in range(KC):
                mk.op("tensor", lambda e, i=i, kc=kc, lb=ln2buf, g=g17_sb,
                      d=pA: e.matmul(d[:, 0:17],
                                     lb[:, kc, 128 * i:128 * i + 128],
                                     g[:, kc, :], start=(kc == 0),
                                     stop=(kc == KC - 1)),
                      mk.s_pe if kc == KC - 1 else None)
            vr = (mk.s_pe, mk.s_pe.v)
            mk.wait("scalar", vr)
            mk.wait("scalar", g_rt_dve)
            g_pA = mk.op("scalar", lambda e, d=pA, l=lgt: e.copy(
                l[:], d[:, 0:17]), mk.s_act)
            mk.wait("vector", g_pA)
            mk.wait("vector", g_rt_act)
            v1 = mk.op("vector", lambda e, l=lgt, r=r_sc: e.reduce_max(
                r[:, 0:1], l[:, 0:16], AX.X), mk.s_dve)
            mk.wait("vector", v1)
            v2 = mk.op("vector", lambda e, r=r_sc: e.tensor_scalar_mul(
                r[:, 1:2], r[:, 0:1], -1.0), mk.s_dve)
            vi1 = mk.op("vector", lambda e, l=lgt, r=r_sc, o=is1t:
                        e.tensor_scalar(o[:], l[:, 0:16], r[:, 0:1], None,
                                        ALU.is_ge), mk.s_dve)
            mk.wait("vector", vi1)
            v3 = mk.op("vector", lambda e, l=lgt, o=is1t, d=l2t:
                       e.scalar_tensor_tensor(d[:], o[:], -1e30, l[:, 0:16],
                                              ALU.mult, ALU.add), mk.s_dve)
            mk.wait("vector", v3)
            v4 = mk.op("vector", lambda e, d=l2t, r=r_sc: e.reduce_max(
                r[:, 2:3], d[:], AX.X), mk.s_dve)
            mk.wait("scalar", v2)
            mk.wait("scalar", v4)
            v5 = mk.op("scalar", lambda e, l=lgt, r=r_sc, o=e1t:
                       e.activation(o[:], l[:, 0:16], AF.Exp,
                                    bias=r[:, 1:2]), mk.s_act)
            v6 = mk.op("scalar", lambda e, r=r_sc: e.activation(
                r[:, 3:4], r[:, 2:3], AF.Exp, bias=r[:, 1:2]), mk.s_act)
            vg = mk.op("scalar", lambda e, i=i, til0=til0, l=lgt, g=g_tm:
                       e.activation(g[:, til0 + i:til0 + i + 1], l[:, 16:17],
                                    AF.Sigmoid), mk.s_act)
            mk.wait("vector", vg)
            v7 = mk.op("vector", lambda e, r=r_sc: e.tensor_scalar_add(
                r[:, 3:4], r[:, 3:4], 1.0), mk.s_dve)
            mk.wait("vector", v7)
            v8 = mk.op("vector", lambda e, r=r_sc: e.reciprocal(
                r[:, 4:5], r[:, 3:4]), mk.s_dve)
            vi2 = mk.op("vector", lambda e, l=lgt, r=r_sc, o=indt:
                        e.tensor_scalar(o[:], l[:, 0:16], r[:, 2:3], None,
                                        ALU.is_ge), mk.s_dve)
            mk.wait("vector", v8)
            mk.wait("vector", vi2)
            v9 = mk.op("vector", lambda e, i=i, til0=til0, o=e1t, r=r_sc,
                       n=indt, c=c_tm: e.scalar_tensor_tensor(
                           c[:, til0 + i, :], o[:], r[:, 4:5], n[:],
                           ALU.mult, ALU.mult), mk.s_dve)
            mk.wait("vector", v9)
            g_rt_dve = v9
            g_rt_act = vg
        router_done = (mk.s_dve, mk.s_dve.v)

        if stop_after == "router":
            if half_idx == 0:
                mk.wait("sync", router_done)
                if "ctm" in taps:
                    mk.op("sync", lambda e: e.dma_start(tap_t["ctm"],
                                                        c_tm[:]), mk.s_st, 16)
                if "gtm" in taps:
                    mk.op("sync", lambda e: e.dma_start(tap_t["gtm"],
                                                        g_tm[:]), mk.s_st, 16)
                mk.wait("sync", (mk.s_st, mk.s_st.v))
                mk.flush()
            return

        G_GU = {"c": 0, "g": [g_pA, 0], "u": [0, 0], "sg": [0, 0]}
        shs = ExitStack()
        asg_sb = [shs.enter_context(sb(f"asg{half_idx}_{i}",
                                       [128, KC, 128], BF16))
                  for i in range(2)]
        asu_sb = [shs.enter_context(sb(f"asu{half_idx}_{i}",
                                       [128, KC, 128], BF16))
                  for i in range(2)]
        asd_sb = shs.enter_context(sb(f"asd_sb{half_idx}", [128, 8, D],
                                      BF16))
        g_asg = [0, 0]
        g_asd = 0
        g_pCD = [0, 0, 0, 0]
        g_sh_pe = 0
        first_block = True
        for (mb0, mb1) in SH_BLOCKS:
            nmb = mb1 - mb0
            for mloc, m in enumerate(range(mb0, mb1)):
                par = m % 2
                mk.wait("gpsimd", g_asg[par])
                mk.op("gpsimd", lambda e, m=m, par=par, a=asg_sb:
                      e.dma_start(a[par][:], dr["asg"][m]), mk.s_w, 16)
                vwu = mk.op("gpsimd", lambda e, m=m, par=par, a=asu_sb:
                            e.dma_start(a[par][:], dr["asu"][m]), mk.s_w, 16)
                mk.wait("tensor", vwu)
                for n in range(4):
                    gb = G_GU["c"] % 2
                    G_GU["c"] += 1
                    bg, bu = (pA, pB) if gb == 0 else (pG, pH)
                    mk.wait("tensor", G_GU["g"][gb])
                    mk.wait("tensor", G_GU["u"][gb])
                    for kc in range(KC):
                        mk.op("tensor", lambda e, kc=kc, n=n, par=par,
                              a=asg_sb, lb=ln2buf, d=bg: e.matmul(
                                  d[:], a[par][:, kc, :],
                                  lb[:, kc, 512 * n:512 * n + 512],
                                  start=(kc == 0), stop=(kc == KC - 1)),
                              mk.s_pe if kc == KC - 1 else None)
                    vg_ = (mk.s_pe, mk.s_pe.v)
                    for kc in range(KC):
                        mk.op("tensor", lambda e, kc=kc, n=n, par=par,
                              a=asu_sb, lb=ln2buf, d=bu: e.matmul(
                                  d[:], a[par][:, kc, :],
                                  lb[:, kc, 512 * n:512 * n + 512],
                                  start=(kc == 0), stop=(kc == KC - 1)),
                              mk.s_pe if kc == KC - 1 else None)
                    vu_ = (mk.s_pe, mk.s_pe.v)
                    mk.wait("scalar", vg_)
                    mk.wait("scalar", G_GU["sg"][gb])
                    vsg = mk.op("scalar", lambda e, d=bg, s=sg2, gb=gb:
                                e.activation(s[gb][:], d[:], AF.Silu),
                                mk.s_act)
                    G_GU["g"][gb] = vsg
                    mk.wait("vector", vu_)
                    mk.wait("vector", vsg)
                    if mloc == 0 and n == 0:
                        mk.wait("vector", g_sh_pe)
                    vh_ = mk.op("vector", lambda e, mloc=mloc, n=n, s=sg2,
                                gb=gb, d=bu, sh=sh_fm: e.tensor_mul(
                                    sh[:, mloc, 512 * n:512 * n + 512],
                                    s[gb][:], d[:]), mk.s_dve)
                    G_GU["u"][gb] = vh_
                    G_GU["sg"][gb] = vh_
                g_asg[par] = (mk.s_pe, mk.s_pe.v)
            sh_done = (mk.s_dve, mk.s_dve.v)
            if "shfm" in taps and half_idx == 0 and mb0 == 0:
                mk.wait("sync", sh_done)
                mk.op("sync", lambda e, s=sh_fm: e.dma_start(
                    tap_t["shfm"], s[:]), mk.s_st, 16)
            mk.wait("gpsimd", g_asd)
            vwd = mk.op("gpsimd", lambda e, mb0=mb0, nmb=nmb, a=asd_sb:
                        e.dma_start(a[:, 0:nmb, :],
                                    dr["asd"][:, mb0:mb0 + nmb, :]),
                        mk.s_w, 16)
            mk.wait("tensor", vwd)
            mk.wait("tensor", sh_done)
            mk.wait("scalar", router_done)
            for i in range(16):
                pd = (pCD[0], pCD[1]) if i % 2 == 0 else (pCD[2], pCD[3])
                mk.wait("tensor", g_pCD[2 * (i % 2)])
                mk.wait("tensor", g_pCD[2 * (i % 2) + 1])
                for half_d in range(2):
                    for mloc in range(nmb):
                        mk.op("tensor", lambda e, i=i, mloc=mloc, nmb=nmb,
                              half_d=half_d, dstp=pd[half_d], sh=sh_fm,
                              a=asd_sb: e.matmul(
                                  dstp[:], sh[:, mloc, 128 * i:128 * i + 128],
                                  a[:, mloc, 512 * half_d:512 * half_d + 512],
                                  start=(mloc == 0), stop=(mloc == nmb - 1)),
                              mk.s_pe if mloc == nmb - 1 else None)
                vdn = (mk.s_pe, mk.s_pe.v)
                mk.wait("vector", vdn)
                mk.wait("vector", router_done)
                for half_d in range(2):
                    sl = slice(512 * half_d, 512 * half_d + 512)
                    if first_block:
                        vac = mk.op("vector", lambda e, i=i, sl=sl, til0=til0,
                                    dstp=pd[half_d], o=out_acc, g=g_tm:
                                    e.tensor_scalar(
                                        o[:, i, sl], dstp[:],
                                        g[:, til0 + i:til0 + i + 1], None,
                                        ALU.mult), mk.s_dve)
                    else:
                        vac = mk.op("vector", lambda e, i=i, sl=sl, til0=til0,
                                    dstp=pd[half_d], o=out_acc, g=g_tm:
                                    e.scalar_tensor_tensor(
                                        o[:, i, sl], dstp[:],
                                        g[:, til0 + i:til0 + i + 1],
                                        o[:, i, sl], ALU.mult, ALU.add),
                                    mk.s_dve)
                    g_pCD[2 * (i % 2) + half_d] = vac
                    mk.wait("vector", vac)
            if "gtm2" in taps and half_idx == 0 and mb0 == 0:
                mk.wait("sync", (mk.s_act, mk.s_act.v))
                mk.op("sync", lambda e: e.dma_start(tap_t["gtm2"], g_tm[:]),
                      mk.s_st, 16)
            if "outacc1" in taps and half_idx == 0 and mb0 == 0:
                mk.wait("sync", (mk.s_dve, mk.s_dve.v))
                mk.op("sync", lambda e: e.dma_start(tap_t["outacc1"],
                                                    out_acc[:]), mk.s_st, 16)
            first_block = False
            g_sh_pe = (mk.s_pe, mk.s_pe.v)
            g_asd = (mk.s_pe, mk.s_pe.v)
        mk.flush()
        shs.close()

        if stop_after == "shared":
            if half_idx == 0 and "outacc" in taps:
                mk.wait("sync", (mk.s_dve, mk.s_dve.v))
                mk.op("sync", lambda e: e.dma_start(tap_t["outacc"],
                                                    out_acc[:]), mk.s_st, 16)
                mk.wait("sync", (mk.s_st, mk.s_st.v))
                mk.flush()
            return

        mos = ExitStack()
        g_he = [0, 0]
        agu_sb = [mos.enter_context(sb(f"agu{half_idx}_{i}",
                                       [128, KC, 2, 128], BF16))
                  for i in range(2)]
        ad_sb = [mos.enter_context(sb(f"ad{half_idx}_{i}", [128, 4, D],
                                      BF16))
                 for i in range(2)]
        g_agu = [0, 0]
        g_ad = [0, 0]
        vda = None
        for e_i in range(NE):
            par = e_i % 2
            for mp in range(4):
                mk.wait("gpsimd", g_agu[par])
                vga = mk.op("gpsimd", lambda e, e_i=e_i, mp=mp, par=par,
                            a=agu_sb: e.dma_start(a[par][:],
                                                  dr["agu"][e_i, mp]),
                            mk.s_w, 16)
                if mp == 0:
                    mk.wait("gpsimd", g_ad[par])
                    vda = mk.op("gpsimd", lambda e, e_i=e_i, par=par,
                                a=ad_sb: e.dma_start(a[par][:],
                                                     dr["ad"][e_i]),
                                mk.s_w, 16)
                mk.wait("tensor", vga)
                for n in range(4):
                    gb = G_GU["c"] % 2
                    G_GU["c"] += 1
                    bg, bu = (pA, pB) if gb == 0 else (pG, pH)
                    mk.wait("tensor", G_GU["g"][gb])
                    mk.wait("tensor", G_GU["u"][gb])
                    for gu in range(2):
                        dstp = bg if gu == 0 else bu
                        for kc in range(KC):
                            mk.op("tensor", lambda e, kc=kc, n=n, gu=gu,
                                  par=par, dstp=dstp, a=agu_sb, lb=ln2buf:
                                  e.matmul(dstp[:], a[par][:, kc, gu, :],
                                           lb[:, kc, 512 * n:512 * n + 512],
                                           start=(kc == 0),
                                           stop=(kc == KC - 1)),
                                  mk.s_pe if kc == KC - 1 else None)
                        if gu == 0:
                            vg_ = (mk.s_pe, mk.s_pe.v)
                        else:
                            vu_ = (mk.s_pe, mk.s_pe.v)
                    mk.wait("scalar", vg_)
                    mk.wait("scalar", G_GU["sg"][gb])
                    vsg = mk.op("scalar", lambda e, d=bg, s=sg2, gb=gb:
                                e.activation(s[gb][:], d[:], AF.Silu),
                                mk.s_act)
                    G_GU["g"][gb] = vsg
                    mk.wait("vector", vu_)
                    mk.wait("vector", vsg)
                    if mp == 0 and n == 0:
                        mk.wait("vector", g_he[e_i % 2])
                    vh_ = mk.op("vector", lambda e, mp=mp, n=n, s=sg2, gb=gb,
                                d=bu, he=h_e2[e_i % 2]: e.tensor_mul(
                                    he[:, mp, 512 * n:512 * n + 512],
                                    s[gb][:], d[:]), mk.s_dve)
                    G_GU["u"][gb] = vh_
                    G_GU["sg"][gb] = vh_
                g_agu[par] = (mk.s_pe, mk.s_pe.v)
            he_done = (mk.s_dve, mk.s_dve.v)
            mk.wait("tensor", vda)
            mk.wait("tensor", he_done)
            for i in range(16):
                pd = (pCD[0], pCD[1]) if i % 2 == 0 else (pCD[2], pCD[3])
                mk.wait("tensor", g_pCD[2 * (i % 2)])
                mk.wait("tensor", g_pCD[2 * (i % 2) + 1])
                for half_d in range(2):
                    for kc in range(4):
                        mk.op("tensor", lambda e, i=i, kc=kc, half_d=half_d,
                              par=par, dstp=pd[half_d], he=h_e2[e_i % 2],
                              a=ad_sb:
                              e.matmul(dstp[:],
                                       he[:, kc, 128 * i:128 * i + 128],
                                       a[par][:, kc,
                                              512 * half_d:512 * half_d + 512],
                                       start=(kc == 0), stop=(kc == 3)),
                              mk.s_pe if kc == 3 else None)
                vdn = (mk.s_pe, mk.s_pe.v)
                mk.wait("vector", vdn)
                for half_d in range(2):
                    sl = slice(512 * half_d, 512 * half_d + 512)
                    vac = mk.op("vector", lambda e, i=i, sl=sl, til0=til0,
                                e_i=e_i, dstp=pd[half_d], o=out_acc, c=c_tm:
                                e.scalar_tensor_tensor(
                                    o[:, i, sl], dstp[:],
                                    c[:, til0 + i, e_i:e_i + 1],
                                    o[:, i, sl], ALU.mult, ALU.add),
                                mk.s_dve)
                    g_pCD[2 * (i % 2) + half_d] = vac
                    mk.wait("vector", vac)
            g_ad[par] = (mk.s_pe, mk.s_pe.v)
            g_he[e_i % 2] = (mk.s_pe, mk.s_pe.v)
        moe_done = (mk.s_dve, mk.s_dve.v)
        mk.flush()
        mos.close()

        if "outacc" in taps and half_idx == 0:
            mk.wait("sync", moe_done)
            mk.op("sync", lambda e: e.dma_start(tap_t["outacc"], out_acc[:]),
                  mk.s_st, 16)

        g_x2 = [0, 0]
        g_out = [0, 0]
        for i in range(16):
            par = i % 2
            mk.wait("sync", g_x2[par])
            vx = mk.op("sync", lambda e, i=i, par=par, x2=x2t, t0=tok0:
                       e.dma_start(x2[par][:],
                                   x2hbm[t0 + 128 * i:t0 + 128 * i + 128, :]),
                       mk.s_x, 16)
            mk.wait("vector", vx)
            mk.wait("vector", g_out[par])
            mk.wait("vector", moe_done)
            vad = mk.op("vector", lambda e, i=i, par=par, o=out_acc, x2=x2t:
                        e.tensor_add(x2[par][:], o[:, i, :], x2[par][:]),
                        mk.s_dve)
            g_x2[par] = vad
            mk.wait("sync", vad)
            vst = mk.op("sync", lambda e, i=i, par=par, x2=x2t, t0=tok0:
                        e.dma_start(y[t0 + 128 * i:t0 + 128 * i + 128, :],
                                    x2[par][:]), mk.s_st, 16)
            g_out[par] = vst
            g_x2[par] = vst
        mk.flush()


_NC_CACHE = {}


def _get_nc():
    if "nc" not in _NC_CACHE:
        _NC_CACHE["nc"] = build_nc()
    return _NC_CACHE["nc"]


def kernel(**inputs):
    w = prep_weights(inputs)
    x = np.ascontiguousarray(np.asarray(inputs["L0_attn_x"],
                                        dtype=np.float32))
    nc = _get_nc()
    in_maps = []
    for c in range(NCORES):
        m = {"x": np.ascontiguousarray(x[2 * c:2 * c + 2].reshape(NTOK, D))}
        m.update(w)
        in_maps.append(m)
    res = run_bass_kernel_spmd(nc, in_maps, core_ids=list(range(NCORES)))
    out = np.empty((B, T, D), dtype=np.float32)
    for c in range(NCORES):
        out[2 * c:2 * c + 2] = res.results[c]["y"].reshape(2, T, D)
    return out
